# version 21
# baseline (speedup 1.0000x reference)
import contextlib
import numpy as np
import ml_dtypes
import concourse.bass as bass
import concourse.mybir as mybir
from concourse.bass_utils import run_bass_kernel_spmd

F32 = mybir.dt.float32
BF16 = mybir.dt.bfloat16
I32 = mybir.dt.int32
AF = mybir.ActivationFunctionType
ALU = mybir.AluOpType
AX = mybir.AxisListType

SEQ, D, H, RNK = 4096, 1024, 8, 32
DHID = 4096
RSQ = 0.08838834764831845
EPS = 1e-5
INV127 = 1.0 / 127.0
CLIPMIN = 1e-5
SQ_SAFE = 1.0404
INV_WN = 1.0 / (DHID * D)

ENGS = ("sync", "tensor", "scalar", "vector", "gpsimd")


class Chan:
    def __init__(self, en, nc, name):
        self.sem = en(nc.semaphore(name))
        self.n = 0
        self.name = name


class Res:
    __slots__ = ("ready", "users")

    def __init__(self):
        self.ready = []
        self.users = []

    def wdeps(self):
        return self.ready + self.users

    def on_write(self, tok):
        assert tok is not None
        self.ready = [tok]
        self.users = []

    def on_read(self, tok):
        if tok is not None:
            self.users.append(tok)


class Prog:
    serial = False

    def __init__(self, nc, en):
        self.nc = nc
        self.streams = {e: [] for e in ENGS}
        self.c = {e: Chan(en, nc, "c" + e[0] + e[1]) for e in ENGS}
        self.d = {e: Chan(en, nc, "d" + e[0] + e[1]) for e in ("sync", "scalar", "gpsimd")}
        self.waited = {e: {} for e in ENGS}
        self.rs = {}

    def R(self, name):
        if name not in self.rs:
            self.rs[name] = Res()
        return self.rs[name]

    def _emit(self, eng, fn, deps, inc_chan, amt):
        if Prog.serial:
            deps = list(deps) + [(c, c.n) for c in self.c.values()] + \
                   [(c, c.n) for c in self.d.values()]
        w = self.waited[eng]
        fdeps = []
        for c, v in deps:
            if v <= 0 or w.get(c.name, 0) >= v:
                continue
            w[c.name] = v
            fdeps.append((c.sem, v))
        tok = None
        if inc_chan is not None:
            inc_chan.n += amt
            tok = (inc_chan, inc_chan.n)

        def thunk(e, fn=fn, fdeps=tuple(fdeps),
                  sem=(inc_chan.sem if inc_chan is not None else None), amt=amt):
            for s, v in fdeps:
                e.wait_ge(s, v)
            r = fn()
            if sem is not None:
                r.then_inc(sem, amt)
        self.streams[eng].append(thunk)
        return tok

    def op(self, eng, fn, deps=(), inc=True):
        return self._emit(eng, fn, deps, self.c[eng] if inc else None, 1)

    def dma(self, eng, out, in_, deps=(), **kw):
        eh = getattr(self.nc, eng)
        return self._emit(eng, lambda: eh.dma_start(out=out, in_=in_, **kw), deps,
                          self.d[eng], 16)

    def all_tokens(self):
        t = [(c, c.n) for c in self.c.values()]
        t += [(c, c.n) for c in self.d.values()]
        return t


def ts(i, n):
    return slice(i * n, (i + 1) * n)


def build_nc(debug=False, serial=False):
    nc = bass.Bass("TRN2", target_bir_lowering=False)

    x_h = nc.dram_tensor("x", [SEQ, D], F32, kind="ExternalInput").ap()
    Uq_h = nc.dram_tensor("Uq", [H, 128, RNK], F32, kind="ExternalInput").ap()
    Vq_h = nc.dram_tensor("Vq", [H, RNK, 128], F32, kind="ExternalInput").ap()
    bq_h = nc.dram_tensor("bq", [H, 128], F32, kind="ExternalInput").ap()
    Uk_h = nc.dram_tensor("Uk", [H, 128, RNK], F32, kind="ExternalInput").ap()
    Vk_h = nc.dram_tensor("Vk", [H, RNK, 128], F32, kind="ExternalInput").ap()
    bk_h = nc.dram_tensor("bk", [H, 128], F32, kind="ExternalInput").ap()
    Uv_h = nc.dram_tensor("Uv", [H, 128, RNK], F32, kind="ExternalInput").ap()
    Vv_h = nc.dram_tensor("Vv", [H, RNK, 128], F32, kind="ExternalInput").ap()
    bv_h = nc.dram_tensor("bv", [H, 128], F32, kind="ExternalInput").ap()
    w1_h = nc.dram_tensor("w1", [DHID, D], F32, kind="ExternalInput").ap()
    w2_h = nc.dram_tensor("w2", [D, DHID], F32, kind="ExternalInput").ap()
    cw_h = nc.dram_tensor("cw", [DHID, 3], F32, kind="ExternalInput").ap()
    cb_h = nc.dram_tensor("cb", [32, 128], F32, kind="ExternalInput").ap()
    identb_h = nc.dram_tensor("identb", [128, 128], BF16, kind="ExternalInput").ap()
    identf_h = nc.dram_tensor("identf", [128, 128], F32, kind="ExternalInput").ap()
    onesrb_h = nc.dram_tensor("onesrb", [1, 128], BF16, kind="ExternalInput").ap()
    onesrf_h = nc.dram_tensor("onesrf", [1, 128], F32, kind="ExternalInput").ap()
    onescb_h = nc.dram_tensor("onescb", [128, 1], BF16, kind="ExternalInput").ap()
    cst_h = nc.dram_tensor("cst", [128, 8], F32, kind="ExternalInput").ap()
    y_h = nc.dram_tensor("y", [SEQ, D], F32, kind="ExternalOutput").ap()

    dkind = "ExternalOutput" if debug else "Internal"
    dbg_d = nc.dram_tensor("dbg_d", [128, 72], F32, kind=dkind).ap() if debug else None
    dbg2_d = nc.dram_tensor("dbg2_d", [1, 640], F32, kind=dkind).ap() if debug else None
    dbg3_d = nc.dram_tensor("dbg3_d", [128, 256], F32, kind=dkind).ap() if debug else None
    dbg4_d = nc.dram_tensor("dbg4_d", [128, 8192], BF16, kind=dkind).ap() if debug else None
    ao_d = nc.dram_tensor("ao_d", [SEQ, D], BF16, kind=dkind).ap()
    x1_d = nc.dram_tensor("x1_d", [SEQ, D], F32, kind=dkind).ap()
    xq1_d = nc.dram_tensor("xq1_d", [D, SEQ + 2], BF16, kind=dkind).ap()
    w1t_d = nc.dram_tensor("w1t_d", [32, 128, 1024], BF16, kind=dkind).ap()
    w2t_d = nc.dram_tensor("w2t_d", [8, 128, 4096], BF16, kind=dkind).ap()

    stck = contextlib.ExitStack()
    en = stck.enter_context

    AB = en(nc.sbuf_tensor("AB", [128, 58368], BF16))
    AFr = en(nc.sbuf_tensor("AF", [128, 17408], F32))
    AI = en(nc.sbuf_tensor("AI", [128, 2304], I32))

    identb = en(nc.sbuf_tensor("s_identb", [128, 128], BF16))
    identf = en(nc.sbuf_tensor("s_identf", [128, 128], F32))
    onesrb = en(nc.sbuf_tensor("s_onesrb", [1, 128], BF16))
    onesrf = en(nc.sbuf_tensor("s_onesrf", [1, 128], F32))
    onescb = en(nc.sbuf_tensor("s_onescb", [128, 1], BF16))
    cst = en(nc.sbuf_tensor("s_cst", [128, 8], F32))
    scal = en(nc.sbuf_tensor("scal", [128, 8], F32))
    wqt = en(nc.sbuf_tensor("wqt", [128, 256], BF16))
    vkt = en(nc.sbuf_tensor("vkt", [128, 256], BF16))
    vvt = en(nc.sbuf_tensor("vvt", [128, 256], BF16))
    uvt = en(nc.sbuf_tensor("uvt", [32, 1024], BF16))
    gk = en(nc.sbuf_tensor("gk", [128, 8], BF16))
    bTq = en(nc.sbuf_tensor("bTq", [128, 8], F32))
    bTv = en(nc.sbuf_tensor("bTv", [128, 8], F32))
    bqb = en(nc.sbuf_tensor("bqb", [128, 8], BF16))
    bvb = en(nc.sbuf_tensor("bvb", [128, 8], BF16))
    wp = en(nc.sbuf_tensor("wp", [128, 256], F32))
    convA = en(nc.sbuf_tensor("convA", [128, 96], F32))
    convB = en(nc.sbuf_tensor("convB", [128, 32], F32))
    dmask = en(nc.sbuf_tensor("dmask", [128, 128], BF16))
    sm = en(nc.sbuf_tensor("sm", [128, 104], F32))
    ewp = en(nc.sbuf_tensor("ewp", [128, 32], F32))
    row = en(nc.sbuf_tensor("row", [1, 640], F32))
    rowb = en(nc.sbuf_tensor("rowb", [1, 640], BF16))

    for i, v in enumerate([EPS, -1.0, 127.0, INV127, CLIPMIN, SQ_SAFE, RSQ, INV_WN]):
        nc.const_aps.register(F32, float(v), cst[:, i:i + 1])
    CST_EPS = cst[:, 0:1]

    pS = en(nc.psum_tensor("pS", [128, 2048], F32))
    pf45 = [en(nc.psum_tensor("pf%d" % i, [128, 512], F32)) for i in (4, 5)]
    pf = [pS[:, 0:512], pS[:, 512:1024], pS[:, 1024:1536], pS[:, 1536:2048],
          pf45[0][:], pf45[1][:]]
    pmS = en(nc.psum_tensor("pmS", [128, 2048], BF16))
    pm = [pmS[:, 0:1024], pmS[:, 1024:2048]]

    Prog.serial = serial
    P = Prog(nc, en)
    R = P.R
    T, Sc, V, G = nc.tensor, nc.scalar, nc.vector, nc.gpsimd

    BK = ["bk%d" % i for i in range(6)] + ["bm0", "bm1"]

    def bank(i):
        return R(BK[i])

    XTM, KST, QST, TVT, XFM, PTR, AOS, BVR = (0, 16384, 24576, 32768, 41216,
                                              45312, 49408, 50432)
    KREP = [XTM, XTM + 8192]
    QREP = [XTM + 4096, XTM + 12288]
    WQT, WQS1, WQS2 = 51456, 53504, 55552
    HS, RHSW, HRW, ATIN, MACC, MACCN, TB, FMS, W1W, W2W, DIAG = (
        0, 16384, 24608, 26664, 28712, 31272, 29224, 32296, 33320, 37416, 45608)
    FMS4 = 0
    WT2, WT1, ALD = 0, 8192, 10240
    XIN, YB, X1B, O2TM, O2F, OUTB, X1T = 0, 2048, 4096, 6144, 10240, 11264, 13312
    IDX, TI, XI = 0, 128, 1152
    ti32 = AI[:, TI: TI + 1024]

    def kap(h):
        return AB[32 * (h % 4): 32 * (h % 4) + 32, KST + 4096 * (h // 4): KST + 4096 * (h // 4) + 4096]

    def qst(h):
        return AB[32 * (h % 4): 32 * (h % 4) + 32, QST + 4096 * (h // 4): QST + 4096 * (h // 4) + 4096]

    def tvt(h):
        return AB[:, TVT + 1056 * h: TVT + 1056 * (h + 1)]

    tp_state = {"i": 0}

    def transpose_evac_bf(src_ap, dst_ap, deps=(), pslice=None):
        i = tp_state["i"] % 2
        tp_state["i"] += 1
        b = bank(6 + i)
        psl = pm[i][0:src_ap.shape[-1], 0:src_ap.shape[0]] if pslice is None else pslice
        pdim = src_ap.shape[0]
        tk = P.op("tensor", lambda: T.transpose(psl, src_ap,
                                                identb[0:pdim, 0:pdim]),
                  deps=list(deps) + b.wdeps() + P.init_ok)
        b.on_write(tk)
        ev = P.op("vector", lambda: V.tensor_copy(dst_ap, psl), deps=[tk])
        b.on_read(ev)
        return ev

    def transpose_evac_bf4(srcs, dst_ap, deps=(), extra_read=None):
        i = tp_state["i"] % 2
        tp_state["i"] += 1
        b = bank(6 + i)
        n = len(srcs)
        tk = None
        for k, src in enumerate(srcs):
            psl = pm[i][:, 128 * k: 128 * (k + 1)]
            tk = P.op("tensor", lambda psl=psl, src=src: T.transpose(psl, src, identb[:]),
                      deps=(list(deps) + b.wdeps() + P.init_ok) if k == 0 else [],
                      inc=(k == n - 1))
        b.on_write(tk)
        if i == 0:
            ev = P.op("vector",
                      lambda dst_ap=dst_ap, n=n: V.tensor_copy(
                          dst_ap, pm[i][:, 0:128 * n].rearrange("p (c f) -> p c f", c=n)),
                      deps=[tk])
        else:
            ev = P.op("scalar",
                      lambda dst_ap=dst_ap, n=n, i=i: Sc.activation(
                          dst_ap, pm[i][:, 0:128 * n].rearrange("p (c f) -> p c f", c=n),
                          AF.Identity, bias=0.0, scale=1.0),
                      deps=[tk])
        b.on_read(ev)
        if extra_read is not None:
            extra_read(ev)
        return ev

    def transpose_evac_f32(src_ap, dst_ap, bids, idx, deps=(), scale=None):
        b = bids[idx % len(bids)]
        psl = pf[b][0:src_ap.shape[-1], 0:src_ap.shape[0]]
        pdim = src_ap.shape[0]
        tk = P.op("tensor", lambda: T.transpose(psl, src_ap, identf[0:pdim, 0:pdim]),
                  deps=list(deps) + bank(b).wdeps() + P.init_ok)
        bank(b).on_write(tk)
        if scale is None:
            ev = P.op("vector", lambda: V.tensor_copy(dst_ap, psl), deps=[tk])
        else:
            ev = P.op("vector", lambda: V.tensor_scalar(dst_ap, psl, scale, None, ALU.mult), deps=[tk])
        bank(b).on_read(ev)
        return ev

    tk1 = P.dma("sync", identb[:], identb_h)
    tk2 = P.dma("sync", identf[:], identf_h)
    tk3 = P.dma("sync", onesrb[:], onesrb_h)
    tk4 = P.dma("sync", onesrf[:], onesrf_h)
    tk5 = P.dma("sync", onescb[:], onescb_h)
    ctok = P.dma("sync", cst[:], cst_h)
    consts_ok = [ctok]
    init_ok = [tk1, tk2, tk3, tk4, tk5, ctok]
    P.init_ok = init_ok
    cw_tok = P.dma("sync", convA[:].rearrange("p (a b) -> p a b", b=3),
                   cw_h.rearrange("(a p) b -> p a b", p=128))
    cb_sb = AFr[0:32, ALD + 768: ALD + 896]
    tok_cb = P.dma("sync", cb_sb, cb_h)
    bld_q = AFr[0:8, ALD + 896: ALD + 1024]
    bld_v = AFr[0:8, ALD + 1024: ALD + 1152]
    tok_bq = P.dma("sync", bld_q, bq_h)
    tok_bv = P.dma("sync", bld_v, bv_h)

    tg = P.op("gpsimd", lambda: G.iota(AI[:, IDX:IDX + 128], pattern=[[1, 128]],
                                       base=0, channel_multiplier=-1))
    P.op("vector", lambda: V.tensor_scalar(dmask[:], AI[:, IDX:IDX + 128], 0.0, None, ALU.is_equal),
         deps=[tg] + consts_ok)

    wt1 = [AFr[:, WT1 + 1024 * j: WT1 + 1024 * (j + 1)] for j in range(2)]
    wt2 = [AFr[:, WT2 + 4096 * j: WT2 + 4096 * (j + 1)] for j in range(2)]
    awt1 = AFr[:, X1T: X1T + 1024]
    awt2 = AFr[:, X1T: X1T + 4096]
    acc1, acc2, tmpr = sm[:, 0:1], sm[:, 1:2], sm[:, 2:3]
    prev_add = P.op("vector", lambda: V.memset(acc1, 0.0))
    prev_add = P.op("vector", lambda: V.memset(acc2, 0.0), deps=[prev_add])
    for t in range(32):
        j = t % 2
        dst = wt1[j]
        d = P.dma("sync", dst, w1_h[ts(t, 128), :], deps=R("wt1_%d" % j).wdeps())
        R("wt1_%d" % j).on_write(d)
        sA_ = P.op("scalar", lambda dst=dst: Sc.activation(awt1, dst, AF.Abs, bias=0.0, scale=1.0),
                   deps=[d] + R("awt").wdeps())
        R("awt").on_write(sA_)
        R("wt1_%d" % j).on_read(sA_)
        v1 = P.op("vector", lambda: V.tensor_reduce(tmpr, awt1, axis=AX.X, op=ALU.add),
                  deps=[sA_] + R("tmpr").wdeps())
        R("tmpr").on_write(v1)
        R("awt").on_read(v1)
        prev_add = P.op("vector", lambda: V.tensor_tensor(acc1, acc1, tmpr, ALU.add),
                        deps=[v1, prev_add])
        R("tmpr").on_read(prev_add)
    for t in range(8):
        j = t % 2
        dst = wt2[j]
        d = P.dma("sync", dst, w2_h[ts(t, 128), :], deps=R("wt2_%d" % j).wdeps())
        R("wt2_%d" % j).on_write(d)
        sA_ = P.op("scalar", lambda dst=dst: Sc.activation(awt2, dst, AF.Abs, bias=0.0, scale=1.0),
                   deps=[d] + R("awt").wdeps())
        R("awt").on_write(sA_)
        R("wt2_%d" % j).on_read(sA_)
        v1 = P.op("vector", lambda: V.tensor_reduce(tmpr, awt2, axis=AX.X, op=ALU.add),
                  deps=[sA_] + R("tmpr").wdeps())
        R("tmpr").on_write(v1)
        R("awt").on_read(v1)
        vlast = P.op("vector", lambda: V.tensor_tensor(acc2, acc2, tmpr, ALU.add),
                     deps=[v1, prev_add])
        prev_add = vlast
        R("tmpr").on_read(prev_add)
    t1 = P.op("tensor", lambda: T.transpose(pf[0][0:1, 0:128], acc1, identf[:]),
              deps=[vlast] + bank(0).wdeps() + init_ok)
    bank(0).on_write(t1)
    t2 = P.op("tensor", lambda: T.transpose(pf[1][0:1, 0:128], acc2, identf[:]),
              deps=[vlast] + bank(1).wdeps())
    bank(1).on_write(t2)
    v1 = P.op("vector", lambda: V.tensor_reduce(row[0:1, 0:1], pf[0][0:1, 0:128], axis=AX.X, op=ALU.add),
              deps=[t1])
    bank(0).on_read(v1)
    v2 = P.op("vector", lambda: V.tensor_reduce(row[0:1, 1:2], pf[1][0:1, 0:128], axis=AX.X, op=ALU.add),
              deps=[t2])
    bank(1).on_read(v2)
    P.op("vector", lambda: V.tensor_scalar(row[0:1, 0:2], row[0:1, 0:2], INV_WN, CLIPMIN, ALU.mult, ALU.max),
         deps=consts_ok + [v1, v2])
    v3 = (P.c["vector"], P.c["vector"].n)
    tb_ = P.op("tensor", lambda: T.matmul(pf[2][:, 0:2], onesrf[:], row[0:1, 0:2], start=True, stop=True),
               deps=[v3] + bank(2).wdeps() + init_ok)
    bank(2).on_write(tb_)
    v4 = P.op("vector", lambda: V.tensor_copy(scal[:, 0:2], pf[2][:, 0:2]), deps=[tb_])
    bank(2).on_read(v4)
    v4 = P.op("vector", lambda: V.reciprocal(scal[:, 2:3], scal[:, 0:1]), deps=[v4])
    v4 = P.op("vector", lambda: V.reciprocal(scal[:, 3:4], scal[:, 1:2]), deps=[v4])
    scal_ok = [v4]
    CW1, CW2, SW1, SW2 = scal[:, 0:1], scal[:, 1:2], scal[:, 2:3], scal[:, 3:4]

    vA = P.op("vector", lambda: V.tensor_scalar(convA[:], convA[:], CW1, None, ALU.mult),
              deps=[cw_tok] + scal_ok)
    convA_ok = [vA]
    vB = transpose_evac_f32(cb_sb, convB[:], (3, 4), 0, deps=[tok_cb])
    convB_ok = [vB]
    vq_ = transpose_evac_f32(bld_q, bTq[:], (3, 4), 1, deps=[tok_bq])
    vv_ = transpose_evac_f32(bld_v, bTv[:], (3, 4), 0, deps=[tok_bv])
    vq2 = P.op("vector", lambda: V.tensor_copy(bqb[:], bTq[:]), deps=[vq_])
    vv2 = P.op("vector", lambda: V.tensor_copy(bvb[:], bTv[:]), deps=[vv_])
    bqb_ok, bvb_ok = [vq2], [vv2]

    for h in range(H):
        uq = AFr[:, ALD + 0: ALD + 32]
        uk = AFr[:, ALD + 32: ALD + 64]
        uv = AFr[:, ALD + 64: ALD + 96]
        vq = AFr[0:32, ALD + 128: ALD + 256]
        vk = AFr[0:32, ALD + 256: ALD + 384]
        vv = AFr[0:32, ALD + 384: ALD + 512]
        rr = R("ald")
        dq = P.dma("sync", uq, Uq_h[h], deps=rr.wdeps())
        dk = P.dma("sync", uk, Uk_h[h])
        dv = P.dma("sync", uv, Uv_h[h])
        d4 = P.dma("sync", vq, Vq_h[h])
        d5 = P.dma("sync", vk, Vk_h[h])
        d6 = P.dma("sync", vv, Vv_h[h])
        rr.on_write(d6)
        uqb = AB[:, XFM + 0: XFM + 32]
        ukb = AB[:, XFM + 32: XFM + 64]
        uvb = AB[:, XFM + 64: XFM + 96]
        vqb = AB[0:32, XFM + 128: XFM + 256]
        vkb = AB[0:32, XFM + 256: XFM + 384]
        vvb = AB[0:32, XFM + 384: XFM + 512]
        gs = AB[0:32, XFM + 512: XFM + 544]
        rx = R("xfm_scratch")
        c1 = P.op("vector", lambda: V.tensor_copy(uqb, uq), deps=[dq] + rx.wdeps())
        c2 = P.op("vector", lambda: V.tensor_copy(ukb, uk), deps=[dk])
        c3 = P.op("vector", lambda: V.tensor_copy(uvb, uv), deps=[dv])
        c4 = P.op("vector", lambda: V.tensor_copy(vqb, vq), deps=[d4])
        c5 = P.op("vector", lambda: V.tensor_copy(vkb, vk), deps=[d5])
        c6 = P.op("vector", lambda: V.tensor_copy(vvb, vv), deps=[d6])
        rr.on_read(c6)
        rx.on_write(c6)
        tG = P.op("tensor", lambda: T.matmul(pf[5][0:32, 0:32], uqb, ukb, start=True, stop=True),
                  deps=[c1, c2] + bank(5).wdeps())
        bank(5).on_write(tG)
        sG = P.op("scalar", lambda: Sc.activation(gs, pf[5][0:32, 0:32], AF.Identity, bias=0.0, scale=RSQ),
                  deps=[tG] + consts_ok)
        bank(5).on_read(sG)
        rx.on_read(sG)
        tW = P.op("tensor", lambda: T.matmul(pf[5][:, 64:96], vqb, gs, start=True, stop=True),
                  deps=[sG, c4] + bank(5).wdeps())
        bank(5).on_write(tW)
        vW = P.op("vector", lambda h=h: V.tensor_copy(wqt[:, ts(h, 32)], pf[5][:, 64:96]), deps=[tW])
        bank(5).on_read(vW)
        R("wqt%d" % h).on_write(vW)
        e1 = transpose_evac_bf(vkb, vkt[:, ts(h, 32)], deps=[c5])
        R("vkt%d" % h).on_write(e1)
        e2 = transpose_evac_bf(vvb, vvt[:, ts(h, 32)], deps=[c6])
        R("vvt%d" % h).on_write(e2)
        e3 = transpose_evac_bf(uvb, uvt[:, ts(h, 128)], deps=[c3])
        R("uvt%d" % h).on_write(e3)
        rx.on_read(e3)
        tg_ = P.op("tensor", lambda h=h: T.matmul(pf[5][0:32, 96:97], ukb, bqb[:, h:h + 1],
                                                  start=True, stop=True),
                   deps=[c2] + bqb_ok + bank(5).wdeps())
        bank(5).on_write(tg_)
        sg_ = P.op("scalar", lambda h=h: Sc.activation(gk[0:32, h:h + 1], pf[5][0:32, 96:97], AF.Identity,
                                                       bias=0.0, scale=RSQ), deps=[tg_])
        bank(5).on_read(sg_)
        R("gk%d" % h).on_write(sg_)

    gk_deps = [R("gk%d" % h).ready[0] for h in range(H)]
    gk_rep = None
    for b_ in range(1, 4):
        gk_rep = P.dma("sync", gk[32 * b_:32 * b_ + 32, :], gk[0:32, :], deps=gk_deps)
    for h in range(H):
        R("gk%d" % h).on_write(gk_rep)

    xtm = [AB[:, XTM + 1024 * j: XTM + 1024 * (j + 1)] for j in range(16)]
    for half in range(2):
        for j in range(16):
            jj = 16 * half + j
            d = P.dma("gpsimd", xtm[j], x_h[ts(jj, 128), :], deps=R("xtm%d" % j).wdeps())
            R("xtm%d" % j).on_write(d)
        for h in range(H):
            xfm = AB[:, XFM + 2048 * (h % 2): XFM + 2048 * (h % 2 + 1)]
            rxf = R("xfm%d" % (h % 2))
            if h < 2:
                rxf.users += R("xfm_scratch").wdeps()
            for g8 in range(2):
                js = [8 * g8 + c4 for c4 in range(8)]
                dep = sum((R("xtm%d" % j).ready for j in js), []) + (rxf.wdeps() if g8 == 0 else [])

                def mark_reads(ev, js=js):
                    for j in js:
                        R("xtm%d" % j).on_read(ev)
                ev = transpose_evac_bf4(
                    [xtm[j][:, ts(h, 128)] for j in js],
                    xfm[:, 1024 * g8: 1024 * (g8 + 1)].rearrange("p (c f) -> p c f", c=8),
                    deps=dep, extra_read=mark_reads)
            rxf.on_write(ev)
            bb = 32 * (h % 4)
            for q in range(4):
                qq = 4 * half + q
                rhs = xfm[:, ts(q, 512)]
                tk = P.op("tensor", lambda h=h, rhs=rhs, bb=bb: T.matmul(
                    pf[0][bb:bb + 32, :], vkt[:, ts(h, 32)], rhs, start=True, stop=True,
                    tile_position=(0, bb)),
                    deps=rxf.ready + R("vkt%d" % h).ready + bank(0).wdeps())
                bank(0).on_write(tk)
                tv = P.op("tensor", lambda h=h, rhs=rhs, bb=bb: T.matmul(
                    pf[1][bb:bb + 32, :], vvt[:, ts(h, 32)], rhs, start=True, stop=True,
                    tile_position=(0, bb)),
                    deps=R("vvt%d" % h).ready + bank(1).wdeps())
                bank(1).on_write(tv)
                tq = P.op("tensor", lambda h=h, rhs=rhs, bb=bb: T.matmul(
                    pf[2][bb:bb + 32, :], wqt[:, ts(h, 32)], rhs, start=True, stop=True,
                    tile_position=(0, bb)),
                    deps=R("wqt%d" % h).ready + bank(2).wdeps())
                bank(2).on_write(tq)
                ev1 = P.op("vector", lambda h=h, qq=qq, bb=bb: V.tensor_copy(kap(h)[:, ts(qq, 512)],
                                                                             pf[0][bb:bb + 32, :]),
                           deps=[tk])
                bank(0).on_read(ev1)
                rxf.on_read(ev1)
                ev2 = P.op("vector", lambda h=h, qq=qq, bb=bb: V.tensor_copy(qst(h)[:, ts(qq, 512)],
                                                                             pf[2][bb:bb + 32, :]),
                           deps=[tq])
                bank(2).on_read(ev2)
                tvrow = AB[bb:bb + 32, PTR + 512 * (q % 2): PTR + 512 * (q % 2) + 512]
                rtv = R("tvrow%d" % (q % 2))
                ev3 = P.op("vector", lambda tvrow=tvrow, bb=bb: V.tensor_copy(tvrow, pf[1][bb:bb + 32, :]),
                           deps=[tv] + rtv.wdeps())
                bank(1).on_read(ev3)
                rtv.on_write(ev3)
                for i in range(4):
                    kt = 4 * qq + i
                    ib = tp_state["i"] % 2
                    tp_state["i"] += 1
                    bmr = bank(6 + ib)
                    psl = pm[ib][0:128, 0:32]
                    tkt = P.op("tensor", lambda tvrow=tvrow, i=i, bb=bb, psl=psl: T.transpose(
                        psl, tvrow[:, ts(i, 128)], identb[bb:bb + 32, bb:bb + 32],
                        tile_position=(bb, 0)),
                        deps=[ev3] + bmr.wdeps() + P.init_ok)
                    bmr.on_write(tkt)
                    rtv.on_read(tkt)
                    evt = P.op("vector", lambda h=h, kt=kt, psl=psl: V.tensor_copy(
                        tvt(h)[:, 33 * kt: 33 * kt + 32], psl), deps=[tkt])
                    bmr.on_read(evt)
                    P.op("vector", lambda h=h, kt=kt: V.memset(tvt(h)[:, 33 * kt + 32: 33 * kt + 33], 1.0),
                         inc=False)
    tvdone = P.op("vector", lambda: V.memset(sm[:, 63:64], 0.0))
    for h in range(H):
        R("kap%d" % h).on_write((P.c["vector"], P.c["vector"].n))
        R("qst%d" % h).on_write((P.c["vector"], P.c["vector"].n))
        R("tvt%d" % h).on_write(tvdone)

    for h in range(H):
        tw = None
        bb = 32 * (h % 4)
        for kt in range(32):
            tw = P.op("tensor", lambda h=h, kt=kt, bb=bb: T.matmul(
                pf[3][:, kt:kt + 1], kap(h)[:, ts(kt, 128)], gk[bb:bb + 32, h:h + 1],
                start=True, stop=True, tile_position=(bb, 0)),
                deps=(R("kap%d" % h).ready + R("gk%d" % h).ready + bank(3).wdeps()) if kt == 0 else [],
                inc=(kt == 31))
        bank(3).on_write(tw)
        sqb = AB[bb:bb + 32, XFM + 0: XFM + 512]
        rsq_ = R("sqb")
        for nm, stash in (("q", qst(h)), ("k", kap(h))):
            base = 8 if nm == "q" else 16
            for q in range(8):
                s1 = P.op("scalar", lambda stash=stash, q=q, sqb=sqb: Sc.activation(
                    sqb, stash[:, ts(q, 512)], AF.Square, bias=0.0, scale=1.0),
                    deps=rsq_.wdeps() + R("xfm0").wdeps() + R("xfm1").wdeps())
                rsq_.on_write(s1)
                tq_ = P.op("tensor", lambda sqb=sqb, bb=bb: T.matmul(
                    pf[4][0:1, :], onescb[bb:bb + 32, 0:1], sqb,
                    start=True, stop=True, tile_position=(bb, 0)),
                           deps=[s1] + bank(4).wdeps() + init_ok)
                bank(4).on_write(tq_)
                rsq_.on_read(tq_)
                vr = P.op("vector", lambda base=base, q=q: V.tensor_reduce(
                    row[0:1, base + q: base + q + 1], pf[4][0:1, :], axis=AX.X, op=ALU.max), deps=[tq_])
                bank(4).on_read(vr)
        vlastr = (P.c["vector"], P.c["vector"].n)
        va = P.op("vector", lambda: V.tensor_reduce(row[0:1, 24:25], row[0:1, 8:16], axis=AX.X, op=ALU.max),
                  deps=[vlastr])
        vb = P.op("vector", lambda: V.tensor_reduce(row[0:1, 25:26], row[0:1, 16:24], axis=AX.X, op=ALU.max),
                  deps=[va])
        v4 = P.op("vector", lambda: V.tensor_tensor(row[0:1, 26:27], row[0:1, 24:25], row[0:1, 25:26], ALU.mult),
                  deps=[vb])
        s2 = P.op("scalar", lambda: Sc.activation(row[0:1, 27:28], row[0:1, 26:27], AF.Sqrt,
                                                  bias=0.0, scale=SQ_SAFE), deps=[v4])
        v5 = P.op("vector", lambda: V.tensor_reduce(sm[:, 4:5], pf[3][:, 0:32], axis=AX.X, op=ALU.max,
                                                    apply_absolute_value=True), deps=[tw])
        bank(3).on_read(v5)
        t2_ = P.op("tensor", lambda: T.transpose(pf[4][0:1, 0:128], sm[:, 4:5], identf[:]),
                   deps=[v5] + bank(4).wdeps())
        bank(4).on_write(t2_)
        v6 = P.op("vector", lambda: V.tensor_reduce(row[0:1, 28:29], pf[4][0:1, 0:128], axis=AX.X, op=ALU.max),
                  deps=[t2_])
        bank(4).on_read(v6)
        v7 = P.op("vector", lambda: V.tensor_tensor(row[0:1, 29:30], row[0:1, 27:28], row[0:1, 28:29], ALU.add),
                  deps=[s2, v6])
        v8 = P.op("vector", lambda: V.tensor_scalar(row[0:1, 30:31], row[0:1, 29:30], -1.0, None, ALU.mult),
                  deps=[v7])
        t3_ = P.op("tensor", lambda: T.matmul(pf[4][:, 0:1], onesrf[:], row[0:1, 30:31], start=True, stop=True),
                   deps=[v8] + bank(4).wdeps())
        bank(4).on_write(t3_)
        v9 = P.op("vector", lambda: V.tensor_copy(sm[:, 5:6], pf[4][:, 0:1]), deps=[t3_])
        bank(4).on_read(v9)
        v10 = P.op("vector", lambda h=h: V.tensor_scalar(wp[:, ts(h, 32)], pf[3][:, 0:32], sm[:, 5:6],
                                                         None, ALU.add), deps=[v9])
        bank(3).on_read(v10)
        R("wp%d" % h).on_write(v10)
        if debug and h == 0:
            P.dma("gpsimd", dbg2_d, row[:], deps=[v10])
        sew = P.op("scalar", lambda h=h: Sc.activation(ewp[:, 0:32], wp[:, ts(h, 32)], AF.Exp,
                                                       bias=0.0, scale=1.0),
                   deps=[v10] + R("ewp").wdeps())
        R("ewp").on_write(sew)
        tvl = None
        for kt in range(32):
            tvl = P.op("vector", lambda h=h, kt=kt: V.tensor_scalar(
                tvt(h)[:, 33 * kt: 33 * kt + 33], tvt(h)[:, 33 * kt: 33 * kt + 33],
                ewp[:, kt: kt + 1], None, ALU.mult),
                deps=([sew] + R("tvt%d" % h).ready) if kt == 0 else [sew], inc=(kt == 31))
        R("ewp").on_read(tvl)
        R("tvt%d" % h).on_write(tvl)

    tern = [AB[:, WQT + 1024 * j: WQT + 1024 * (j + 1)] for j in range(2)]
    st1b = [AB[:, WQS1 + 1024 * j: WQS1 + 1024 * (j + 1)] for j in range(2)]
    st2b = AB[:, WQS2: WQS2 + 2048]

    wq_defer = []
    w2_dtok = {}

    def emit_w1_tile(t):
        j = t % 2
        dst = wt1[j]
        d = P.dma("sync", dst, w1_h[ts(t, 128), :], deps=R("wt1_%d" % j).wdeps())
        R("wt1_%d" % j).on_write(d)
        v1 = P.op("vector", lambda dst=dst: V.tensor_scalar(ti32, dst, SW1, None, ALU.mult),
                  deps=[d] + scal_ok + R("ti32").wdeps())
        R("ti32").on_write(v1)
        R("wt1_%d" % j).on_read(v1)
        v2 = P.op("vector", lambda j=j: V.tensor_scalar(tern[j], ti32, 1.0, -1.0, ALU.min, ALU.max),
                  deps=[v1] + R("tern%d" % j).wdeps())
        R("tern%d" % j).on_write(v2)
        R("ti32").on_read(v2)

        def part_b(t=t, j=j, v2=v2):
            rst = R("st1_%d" % j)
            ev = transpose_evac_bf4(
                [tern[j][:, ts(c4, 128)] for c4 in range(8)],
                st1b[j].rearrange("p (c f) -> p c f", c=8),
                deps=[v2] + rst.wdeps(),
                extra_read=R("tern%d" % j).on_read)
            rst.on_write(ev)
            d2 = P.dma("gpsimd", w1t_d[t], st1b[j], deps=[ev])
            rst.on_read(d2)
        wq_defer.append(part_b)

    def emit_w2_cc(t, cc):
        j2 = t % 2
        dst = wt2[j2]
        if cc == 0:
            d0 = P.dma("sync", dst, w2_h[ts(t, 128), :], deps=R("wt2_%d" % j2).wdeps())
            R("wt2_%d" % j2).on_write(d0)
            w2_dtok[t] = d0
        d = w2_dtok[t]
        j = cc % 2
        v1 = P.op("vector", lambda dst=dst, cc=cc: V.tensor_scalar(ti32, dst[:, ts(cc, 1024)], SW2,
                                                                   None, ALU.mult),
                  deps=[d] + scal_ok + R("ti32").wdeps())
        R("ti32").on_write(v1)
        R("wt2_%d" % j2).on_read(v1)
        v2 = P.op("vector", lambda j=j: V.tensor_scalar(tern[j], ti32, 1.0, -1.0, ALU.min, ALU.max),
                  deps=[v1] + R("tern%d" % j).wdeps())
        R("tern%d" % j).on_write(v2)
        R("ti32").on_read(v2)

        def part_b(t=t, cc=cc, j=j, v2=v2):
            rst = R("st2")
            base = 1024 * (cc % 2)
            ev = transpose_evac_bf4(
                [tern[j][:, ts(c4, 128)] for c4 in range(8)],
                st2b[:, base: base + 1024].rearrange("p (c f) -> p c f", c=8),
                deps=[v2] + (rst.wdeps() if cc % 2 == 0 else []),
                extra_read=R("tern%d" % j).on_read)
            if cc % 2 == 1:
                rst.on_write(ev)
                d2 = P.dma("gpsimd", w2t_d[t, :, 2048 * (cc // 2): 2048 * (cc // 2 + 1)], st2b, deps=[ev])
                rst.on_read(d2)
        wq_defer.append(part_b)

    wq_work = []
    for t in range(8):
        for cc in range(4):
            wq_work.append(lambda t=t, cc=cc: emit_w2_cc(t, cc))
    for t in range(32):
        wq_work.append(lambda t=t: emit_w1_tile(t))
    wq_pos = [0]

    def emit_wquant_piece(n=1):
        for _ in range(n):
            if wq_pos[0] < len(wq_work):
                while len(wq_defer) > 1:
                    wq_defer.pop(0)()
                wq_work[wq_pos[0]]()
                wq_pos[0] += 1
                if wq_pos[0] == 32:
                    while wq_defer:
                        wq_defer.pop(0)()
                    w2_done[0] = P.all_tokens()
                if wq_pos[0] == len(wq_work):
                    while wq_defer:
                        wq_defer.pop(0)()
                    wq_done[0] = P.all_tokens()

    xq1_tok = {}
    x1_tok = {}
    twob_defer = []
    wq_done = [[]]
    w2_done = [[]]

    def emit_2a_subtile(stt):
            j = stt % 2
            xin = AFr[:, XIN + 1024 * j: XIN + 1024 * (j + 1)]
            yb = AFr[:, YB + 1024 * j: YB + 1024 * (j + 1)]
            x1b = AFr[:, X1B + 1024 * j: X1B + 1024 * (j + 1)]
            atin = AB[:, ATIN + 1024 * j: ATIN + 1024 * (j + 1)]
            tbb = AB[:, TB + 1024 * j: TB + 1024 * (j + 1)]
            fms4 = AB[:, 20496: 20496 + 4096]
            adeps = [aod_tok[(hh, stt // 4)] for hh in range(H)]
            dx = P.dma("sync", xin, x_h[ts(stt, 128), :], deps=R("xin%d" % j).wdeps() + w2_done[0])
            R("xin%d" % j).on_write(dx)
            da = P.dma("gpsimd", atin, ao_d[ts(stt, 128), :], deps=R("atin%d" % j).wdeps() + adeps)
            R("atin%d" % j).on_write(da)
            v1 = P.op("vector", lambda xin=xin, atin=atin, yb=yb: V.tensor_tensor(yb, xin, atin, ALU.add),
                      deps=[dx, da] + R("yb%d" % j).wdeps())
            R("xin%d" % j).on_read(v1)
            R("atin%d" % j).on_read(v1)
            R("yb%d" % j).on_write(v1)
            bs = sm[:, 8 + 12 * j: 20 + 12 * j]
            mv = sm[:, 32 + 2 * j: 34 + 2 * j]
            vb1 = P.op("vector", lambda yb=yb, bs=bs: V.bn_stats(bs[:, 0:6], yb[:, 0:512]),
                       deps=[v1] + R("bs%d" % j).wdeps())
            vb2 = P.op("vector", lambda yb=yb, bs=bs: V.bn_stats(bs[:, 6:12], yb[:, 512:1024]), deps=[vb1])
            v4 = P.op("vector", lambda bs=bs, mv=mv: V.bn_aggr(mv, bs), deps=[vb2] + R("mv%d" % j).wdeps())
            R("bs%d" % j).on_write(vb2)
            R("bs%d" % j).on_read(v4)
            R("mv%d" % j).on_write(v4)
            sd = sm[:, 36 + j: 37 + j]
            s1_ = P.op("scalar", lambda mv=mv, sd=sd: Sc.activation(sd, mv[:, 1:2], AF.Sqrt, bias=CST_EPS,
                                                                    scale=1.0),
                       deps=[v4] + R("sd%d" % j).wdeps())
            R("mv%d" % j).on_read(s1_)
            R("sd%d" % j).on_write(s1_)
            rstd = sm[:, 38 + j: 39 + j]
            v5 = P.op("vector", lambda sd=sd, rstd=rstd: V.reciprocal(rstd, sd),
                      deps=[s1_] + R("rstd%d" % j).wdeps())
            R("sd%d" % j).on_read(v5)
            R("rstd%d" % j).on_write(v5)
            nmr = sm[:, 40 + j: 41 + j]
            v6 = P.op("vector", lambda mv=mv, rstd=rstd, nmr=nmr: V.scalar_tensor_tensor(
                nmr, mv[:, 0:1], -1.0, rstd, ALU.mult, ALU.mult),
                deps=[v5] + R("nmr%d" % j).wdeps())
            R("nmr%d" % j).on_write(v6)
            s2_ = P.op("scalar", lambda yb=yb, x1b=x1b, rstd=rstd, nmr=nmr: Sc.activation(
                x1b, yb, AF.Identity, bias=nmr, scale=rstd),
                deps=[v6] + R("x1b%d" % j).wdeps())
            R("yb%d" % j).on_read(s2_)
            R("rstd%d" % j).on_read(s2_)
            R("nmr%d" % j).on_read(s2_)
            R("x1b%d" % j).on_write(s2_)
            dx1 = P.dma("sync", x1_d[ts(stt, 128), :], x1b, deps=[s2_])
            R("x1b%d" % j).on_read(dx1)
            x1_tok[stt] = dx1
            m1 = sm[:, 42 + j: 43 + j]
            v7 = P.op("vector", lambda x1b=x1b, m1=m1: V.tensor_reduce(m1, x1b, axis=AX.X, op=ALU.max,
                                                                       apply_absolute_value=True),
                      deps=[s2_] + R("m1%d" % j).wdeps())
            R("m1%d" % j).on_write(v7)
            v8c = P.op("vector", lambda m1=m1: V.tensor_scalar(m1, m1, CLIPMIN, None, ALU.max), deps=[v7])
            rc = sm[:, 44 + j: 45 + j]
            v9c = P.op("vector", lambda m1=m1, rc=rc: V.reciprocal(rc, m1), deps=[v8c])
            s1c = sm[:, 46 + j: 47 + j]
            rs1 = sm[:, 48 + j: 49 + j]
            v10c = P.op("vector", lambda rc=rc, s1c=s1c: V.tensor_scalar(s1c, rc, 127.0, None, ALU.mult),
                        deps=[v9c])
            v11c = P.op("vector", lambda m1=m1, rs1=rs1: V.tensor_scalar(rs1, m1, INV127, None, ALU.mult),
                        deps=[v10c])
            R("m1%d" % j).on_read(v11c)
            v12 = P.op("vector", lambda x1b=x1b, s1c=s1c: V.tensor_scalar(ti32, x1b, s1c, None, ALU.mult),
                       deps=[v11c] + R("ti32").wdeps())
            R("ti32").on_write(v12)
            R("x1b%d" % j).on_read(v12)
            v13 = P.op("vector", lambda tbb=tbb, rs1=rs1: V.tensor_scalar(tbb, ti32, rs1, None, ALU.mult),
                       deps=[v12] + R("tb%d" % j).wdeps())
            R("ti32").on_read(v13)
            R("tb%d" % j).on_write(v13)
            def part_b(stt=stt, j=j, tbb=tbb, fms4=fms4, v13=v13):
                rfm = R("fms4")
                ev = None
                fmr = fms4.rearrange("p (c s) -> p c s", c=8)
                dst = fmr[:, :, 128 * (stt % 4): 128 * (stt % 4) + 128]
                ev = transpose_evac_bf4(
                    [tbb[:, ts(c4, 128)] for c4 in range(8)],
                    dst,
                    deps=[v13] + (rfm.wdeps() if stt % 4 == 0 else []),
                    extra_read=R("tb%d" % j).on_read)
                if stt % 4 == 0:
                    rfm.on_write(ev)
                else:
                    rfm.ready = [ev]
                if stt % 4 == 3:
                    tt_ = stt // 4
                    dxq = P.dma("gpsimd",
                                xq1_d[:, 1 + 512 * tt_: 1 + 512 * (tt_ + 1)].rearrange("(a p) f -> p a f", p=128),
                                fms4.rearrange("p (a f) -> p a f", a=8),
                                deps=[ev])
                    rfm.on_read(dxq)
                    xq1_tok[tt_] = dxq
            twob_defer.append(part_b)


    twoa_next = [0]

    def advance_2a(n):
        for _ in range(n):
            if twoa_next[0] < 32:
                while len(twob_defer) > 1:
                    twob_defer.pop(0)()
                emit_2a_subtile(twoa_next[0])
                twoa_next[0] += 1

    def ensure_2a_tile(t):
        while twoa_next[0] < 4 * (t + 1):
            emit_2a_subtile(twoa_next[0])
            twoa_next[0] += 1
        while t not in xq1_tok and twob_defer:
            twob_defer.pop(0)()

    pt_ring = [AB[:, PTR + 1024 * i: PTR + 1024 * (i + 1)] for i in range(4)]
    xtm_users = [(P.c["tensor"], P.c["tensor"].n), (P.c["vector"], P.c["vector"].n)]

    def emit_reps(h):
        sl = h % 2
        deps = (R("rep%d" % sl).wdeps() + R("kap%d" % h).ready + R("qst%d" % h).ready +
                (xtm_users if h < 2 else []))
        tok = None
        for b_ in range(4):
            tok = P.dma("sync", AB[32 * b_: 32 * b_ + 32, KREP[sl]: KREP[sl] + 4096], kap(h),
                        deps=deps if b_ == 0 else [])
        for b_ in range(4):
            tok = P.dma("sync", AB[32 * b_: 32 * b_ + 32, QREP[sl]: QREP[sl] + 4096], qst(h))
        R("rep%d" % sl).on_write(tok)

    def emit_bvrep(h):
        sl = h % 2
        e4 = transpose_evac_bf(bvb[:, h:h + 1], rowb[0:1, 0:128], deps=bvb_ok)
        tb1 = P.op("tensor", lambda: T.matmul(pf[5][:, 0:128], onesrb[:], rowb[0:1, 0:128],
                                              start=True, stop=True),
                   deps=[e4] + bank(5).wdeps() + init_ok)
        bank(5).on_write(tb1)
        v7 = None
        for qs in range(4):
            v7 = P.op("vector", lambda sl=sl, qs=qs: V.tensor_copy(
                AB[:, BVR + 512 * sl + 128 * qs: BVR + 512 * sl + 128 * (qs + 1)], pf[5][:, 0:128]),
                deps=([tb1] + R("bvr%d" % sl).wdeps()) if qs == 0 else [tb1])
            bank(5).on_read(v7)
        R("bvr%d" % sl).on_write(v7)

    aod_tok = {}
    emit_reps(0)
    wtok = None
    for i in range(16):
        b_ = i % 4
        wtok = P.op("tensor", lambda b_=b_: T.matmul(
            pS[:, 512 * b_: 512 * (b_ + 1)], identb[:], AB[:, XTM: XTM + 512],
            start=True, stop=True),
            deps=(bank(0).wdeps() + bank(1).wdeps() + bank(2).wdeps() + bank(3).wdeps() + init_ok)
            if i == 0 else [], inc=(i == 15))
    if wtok is not None:
        for b_ in range(4):
            bank(b_).on_write(wtok)

    for h in range(H):
        emit_bvrep(h)
        sl = h % 2
        krep = AB[:, KREP[sl]: KREP[sl] + 4096]
        qrep = AB[:, QREP[sl]: QREP[sl] + 4096]
        last_score = None
        norm_pend = []
        for q in range(8):
            if q == 1 and h + 1 < H:
                emit_reps(h + 1)
            if h == 7 and q >= 2:
                advance_2a(4)
            last_pv = None
            prev_exp = None

            def do_pv_group(kg, etok):
                tpv = None
                for i in range(4):
                    kt = 4 * kg + i
                    pts = pt_ring[2 * (kg % 2) + i // 2][:, 512 * (i % 2): 512 * (i % 2 + 1)]
                    tpv = P.op("tensor", lambda h=h, kt=kt, pts=pts: T.matmul(
                        pf[4][0:33, :], tvt(h)[:, 33 * kt: 33 * kt + 33], pts,
                        start=(kt == 0), stop=(kt == 31)),
                        deps=([etok] + R("tvt%d" % h).ready +
                              (bank(4).wdeps() if kt == 0 else [])) if i == 0 else [etok],
                        inc=(i == 3))
                R("ptg%d" % (kg % 2)).on_read(tpv)
                return tpv

            for kg in range(8):
                tlh = [None, None]
                seh = [None, None]
                for half in range(2):
                    bks = (0, 1) if half == 0 else (2, 3)
                    bdeps = bank(bks[0]).wdeps() + bank(bks[1]).wdeps()
                    tl = None
                    for i2 in range(2):
                        i = 2 * half + i2
                        kt = 4 * kg + i
                        sli = pS[:, 512 * i: 512 * (i + 1)]
                        tl = P.op("tensor", lambda q=q, kt=kt, i=i, sli=sli, krep=krep, qrep=qrep: T.matmul(
                            sli, krep[32 * i: 32 * (i + 1), ts(kt, 128)],
                            qrep[32 * i: 32 * (i + 1), 512 * q: 512 * (q + 1)],
                            start=True, stop=True, tile_position=(32 * i, 0)),
                            deps=(bdeps + R("rep%d" % sl).ready) if i2 == 0 else [],
                            inc=(i2 == 1))
                    tlh[half] = tl
                    se = P.op("scalar", lambda kg=kg, half=half: Sc.activation(
                        AB[:, PTR + 2048 * (kg % 2) + 1024 * half: PTR + 2048 * (kg % 2) + 1024 * (half + 1)],
                        pS[:, 1024 * half: 1024 * (half + 1)], AF.Exp, bias=0.0, scale=1.0),
                        deps=[tl] + (R("ptg%d" % (kg % 2)).wdeps() if half == 0 else []))
                    seh[half] = se
                    for b_ in bks:
                        bank(b_).on_write(tl)
                        bank(b_).on_read(se)
                last_score = tlh[1]
                se = seh[1]
                R("ptg%d" % (kg % 2)).on_write(se)
                if prev_exp is not None:
                    last_pv = do_pv_group(kg - 1, prev_exp)
                prev_exp = se
                if kg == 1 and norm_pend:
                    norm_pend.pop(0)()
            last_pv = do_pv_group(7, prev_exp)
            bank(4).on_write(last_pv)
            v1 = P.op("vector", lambda: V.reciprocal(row[0:1, 32:544], pf[4][32:33, :]), deps=[last_pv])
            bank(4).on_read(v1)
            o1f = AFr[0:32, ALD + 512 * (q % 2): ALD + 512 * (q % 2) + 512]
            ro1 = R("o1f%d" % (q % 2))
            ve = P.op("vector", lambda o1f=o1f: V.tensor_copy(o1f, pf[4][0:32, :]),
                      deps=[last_pv] + ro1.wdeps())
            bank(4).on_read(ve)
            ro1.on_write(ve)

            def do_norm(h=h, q=q, sl=sl, o1f=o1f, ro1=ro1, v1=v1, ve=ve):
                v2 = P.op("scalar", lambda: Sc.activation(rowb[0:1, 128:640], row[0:1, 32:544],
                                                          AF.Identity, bias=0.0, scale=1.0), deps=[v1])
                tb1 = P.op("tensor", lambda: T.matmul(pf[5][0:32, :], onesrb[0:1, 0:32], rowb[0:1, 128:640],
                                                      start=True, stop=True),
                           deps=[v2] + bank(5).wdeps())
                bank(5).on_write(tb1)
                rbs = AB[0:32, XFM + 0: XFM + 512]
                o1n = AB[0:32, XFM + 512: XFM + 1024]
                v3 = P.op("scalar", lambda rbs=rbs: Sc.activation(rbs, pf[5][0:32, :],
                                                                 AF.Identity, bias=0.0, scale=1.0),
                          deps=[tb1] + R("sqb").wdeps())
                bank(5).on_read(v3)
                R("sqb").on_write(v3)
                v4 = P.op("vector", lambda o1n=o1n, o1f=o1f, rbs=rbs: V.tensor_tensor(o1n, o1f, rbs, ALU.mult),
                          deps=[v3, ve])
                ro1.on_read(v4)
                R("sqb").on_read(v4)
                ta = None
                for qs in range(4):
                    ta = P.op("tensor", lambda h=h, qs=qs, o1n=o1n: T.matmul(
                        pf[5][:, ts(qs, 128)], o1n[:, ts(qs, 128)], uvt[:, ts(h, 128)],
                        start=True, stop=True),
                        deps=([v4] + R("uvt%d" % h).ready + bank(5).wdeps()) if qs == 0 else [],
                        inc=(qs == 3))
                bank(5).on_write(ta)
                aos = AB[:, AOS + 512 * (q % 2): AOS + 512 * (q % 2 + 1)]
                ras = R("aos%d" % (q % 2))
                v5 = P.op("vector", lambda aos=aos, sl=sl: V.tensor_tensor(
                    aos, pf[5][:], AB[:, BVR + 512 * sl: BVR + 512 * (sl + 1)], ALU.add),
                    deps=[ta] + R("bvr%d" % sl).ready + ras.wdeps())
                bank(5).on_read(v5)
                ras.on_write(v5)
                dq_ = P.dma("gpsimd",
                            ao_d[512 * q: 512 * (q + 1), ts(h, 128)].rearrange("(a p) f -> p a f", p=128),
                            aos.rearrange("p (a f) -> p a f", a=4),
                            deps=[v5])
                ras.on_read(dq_)
                aod_tok[(h, q)] = dq_
            norm_pend.append(do_norm)
            emit_wquant_piece(1)
        while norm_pend:
            norm_pend.pop(0)()
        R("rep%d" % sl).on_read(last_score)
        R("bvr%d" % sl).on_read((P.c["vector"], P.c["vector"].n))

    emit_wquant_piece(len(wq_work))

    bar = P.all_tokens()
    dtoks = {}
    for ht in range(32):
        vd = None
        for j3 in range(3):
            vd = P.op("vector", lambda ht=ht, j3=j3: V.tensor_scalar(
                AB[:, DIAG + 128 * (3 * ht + j3): DIAG + 128 * (3 * ht + j3 + 1)],
                dmask[:], convA[:, 3 * ht + j3: 3 * ht + j3 + 1], None, ALU.mult),
                deps=(bar + convA_ok) if (ht == 0 and j3 == 0) else [], inc=(j3 == 2))
        dtoks[ht] = vd
    zc = AB[:, MACC: MACC + 1]
    vz = P.op("vector", lambda: V.memset(zc, 0.0), deps=bar)
    zpads = []
    for c in range(8):
        zpads.append(P.dma("gpsimd", xq1_d[ts(c, 128), 0:1], zc, deps=[vz],
                           allow_slow_non_contiguous=True))
        zpads.append(P.dma("gpsimd", xq1_d[ts(c, 128), SEQ + 1: SEQ + 2], zc, deps=[vz],
                           allow_slow_non_contiguous=True))

    bar2 = P.all_tokens()

    hsb = AB[:, HS: HS + 16384]
    macc = AB[:, MACC: MACC + 512]
    maccn = AB[:, MACCN: MACCN + 512]

    def load_w1win(w):
        jw = w % 2
        dst = AB[:, W1W + 2048 * jw: W1W + 2048 * (jw + 1)]
        d = P.dma("scalar", dst.rearrange("p (a f) -> p a f", a=2),
                  w1t_d[2 * w: 2 * w + 2].rearrange("a p f -> p a f"),
                  deps=R("w1w%d" % jw).wdeps() + bar2)
        R("w1w%d" % jw).on_write(d)
        return d

    def load_w2win(w):
        jw = w % 4
        dst = AB[:, W2W + 2048 * jw: W2W + 2048 * (jw + 1)]
        d = P.dma("scalar", dst, w2t_d[w // 2, :, 2048 * (w % 2): 2048 * (w % 2 + 1)],
                  deps=R("w2w%d" % jw).wdeps() + bar2)
        R("w2w%d" % jw).on_write(d)
        return d

    rhsw = AB[:, RHSW: RHSW + 4112]

    def emit_rhsw_load(T):
        rrh = R("rhsw0")
        rdeps = rrh.wdeps() + bar2 + zpads + \
            [xq1_tok[t_] for t_ in (T - 1, T, T + 1) if t_ in xq1_tok]
        rh_engs = ("sync", "gpsimd", "scalar")
        rh_last = {}
        for c in range(8):
            e_ = rh_engs[c % 3]
            rh_last[e_] = P.dma(e_, rhsw[:, 514 * c: 514 * (c + 1)],
                                xq1_d[ts(c, 128), 512 * T: 512 * T + 514],
                                deps=rdeps if c < 3 else [])
        rrh.ready = list(rh_last.values())
        rrh.users = []

    ensure_2a_tile(1)
    emit_rhsw_load(0)
    wpre1, wpre2 = {}, {}
    for tt in range(8):
        ensure_2a_tile(min(tt + 1, 7))
        j = 0
        rrh = R("rhsw0")
        for qs in range(4):
            x1t = AFr[:, X1T + 1024 * qs: X1T + 1024 * (qs + 1)]
            d = P.dma("sync", x1t, x1_d[512 * tt + 128 * qs: 512 * tt + 128 * (qs + 1), :],
                      deps=R("x1t%d" % qs).wdeps() + bar2 + [x1_tok[4 * tt + qs]])
            R("x1t%d" % qs).on_write(d)
        wtoks1 = dict(wpre1)
        wpre1.clear()
        for w in (0, 1):
            if w not in wtoks1:
                wtoks1[w] = load_w1win(w)
        wtoks2 = dict(wpre2)
        wpre2.clear()
        for w in (0, 1):
            if w not in wtoks2:
                wtoks2[w] = load_w2win(w)

        vm = None
        conv_pend = []

        def do_conv(ht, e1, e2, hr):
            bc = 4 + (ht % 2)
            tcv = None
            for j3 in range(3):
                tcv = P.op("tensor", lambda bc=bc, ht=ht, j3=j3, hr=hr: T.matmul(
                    pf[bc][:], AB[:, DIAG + 128 * (3 * ht + j3): DIAG + 128 * (3 * ht + j3 + 1)],
                    hr[:, j3: j3 + 512], start=(j3 == 0), stop=(j3 == 2)),
                    deps=([e1, e2, dtoks[ht]] + bank(bc).wdeps()) if j3 == 0 else [],
                    inc=(j3 == 2))
            bank(bc).on_write(tcv)
            R("hr%d" % (ht % 4)).on_read(tcv)
            ssl = P.op("scalar", lambda ht=ht, bc=bc: Sc.activation(
                hsb[:, ts(ht, 512)], pf[bc][:], AF.Silu, bias=convB[:, ht: ht + 1], scale=1.0),
                deps=[tcv] + convB_ok + (R("hs").wdeps() if ht == 0 else []))
            bank(bc).on_read(ssl)
            if ht == 0:
                R("hs").on_write(ssl)
            return ssl

        def do_macc(hti, ssl):
            global_unused = None
            if hti == 0:
                v_ = P.op("vector", lambda hti=hti: V.tensor_copy(macc, hsb[:, ts(hti, 512)]),
                          deps=[ssl] + R("macc").wdeps())
                R("macc").on_write(v_)
                v_ = P.op("vector", lambda hti=hti: V.tensor_copy(maccn, hsb[:, ts(hti, 512)]),
                          deps=[ssl, v_] + R("maccn").wdeps())
                R("maccn").on_write(v_)
            else:
                v_ = P.op("vector", lambda hti=hti: V.tensor_tensor(
                    macc, macc, hsb[:, ts(hti, 512)], ALU.max), deps=[ssl, prev_vm[0]])
                v_ = P.op("vector", lambda hti=hti: V.tensor_tensor(
                    maccn, maccn, hsb[:, ts(hti, 512)], ALU.min), deps=[ssl, v_])
            prev_vm[0] = v_
            return v_

        prev_vm = [None]
        for ht in range(32):
            if ht in (8, 16, 24):
                advance_2a(1)
            jp = ht % 2
            b0, b1 = (0, 1) if jp == 0 else (2, 3)
            w1w = AB[:, W1W + 2048 * ((ht // 2) % 2): W1W + 2048 * ((ht // 2) % 2 + 1)]
            lhs_off = 1024 * (ht % 2)
            dep0 = (rrh.ready + [wtoks1[ht // 2]] + bank(b0).wdeps() + bank(b1).wdeps())
            tmm = None
            for half in range(2):
                bb = b0 if half == 0 else b1
                for c in range(8):
                    tmm = P.op("tensor", lambda bb=bb, w1w=w1w, lhs_off=lhs_off, c=c, half=half,
                               rhsw=rhsw: T.matmul(
                        pf[bb][:, 0:257],
                        w1w[:, lhs_off + 128 * c: lhs_off + 128 * (c + 1)],
                        rhsw[:, 514 * c + 257 * half: 514 * c + 257 * half + 257],
                        start=(c == 0), stop=(c == 7)),
                        deps=dep0 if (half == 0 and c == 0) else [],
                        inc=(c == 7))
                bank(bb).on_write(tmm)
            rrh.on_read(tmm)
            R("w1w%d" % ((ht // 2) % 2)).on_read(tmm)
            if ht % 2 == 1 and ht // 2 + 2 < 16:
                wtoks1[ht // 2 + 2] = load_w1win(ht // 2 + 2)
            hr = AB[:, HRW + 514 * (ht % 4): HRW + 514 * (ht % 4 + 1)]
            rhr = R("hr%d" % (ht % 4))
            e1 = P.op("scalar", lambda hr=hr, b0=b0: Sc.activation(hr[:, 0:257], pf[b0][:, 0:257],
                                                                   AF.Identity, bias=0.0, scale=1.0),
                      deps=[tmm] + rhr.wdeps())
            bank(b0).on_read(e1)
            e2 = P.op("vector", lambda hr=hr, b1=b1: V.tensor_copy(hr[:, 257:514], pf[b1][:, 0:257]),
                      deps=[tmm] + rhr.wdeps())
            bank(b1).on_read(e2)
            rhr.on_write(e2)
            rhr.users.append(e1)
            conv_pend.append((ht, e1, e2, hr))
            if len(conv_pend) > 1:
                hti, e1i, e2i, hri = conv_pend.pop(0)
                ssl = do_conv(hti, e1i, e2i, hri)
                vm = do_macc(hti, ssl)

        while conv_pend:
            hti, e1i, e2i, hri = conv_pend.pop(0)
            ssl = do_conv(hti, e1i, e2i, hri)
            vm = do_macc(hti, ssl)

        vneg = P.op("vector", lambda: V.tensor_scalar(maccn, maccn, -1.0, None, ALU.mult), deps=[vm])
        vm = P.op("vector", lambda: V.tensor_tensor(macc, macc, maccn, ALU.max), deps=[vneg])
        R("macc").on_write(vm)
        R("maccn").on_read(vm)
        for qs in range(4):
            i2 = qs % 2
            b = bank(6 + i2)
            psl = pm[i2][0:128, 0:128]
            tmc = P.op("tensor", lambda qs=qs, psl=psl: T.transpose(psl, macc[:, ts(qs, 128)], identb[:]),
                       deps=([vm] if qs < 2 else []) + b.wdeps())
            b.on_write(tmc)
            if qs == 3:
                R("macc").on_read(tmc)
            m2 = sm[:, 50 + qs: 51 + qs]
            v1 = P.op("vector", lambda m2=m2, psl=psl: V.tensor_reduce(m2, psl, axis=AX.X, op=ALU.max),
                      deps=[tmc])
            b.on_read(v1)
            v2_ = P.op("vector", lambda m2=m2: V.tensor_scalar(m2, m2, CLIPMIN, None, ALU.max), deps=[v1])
            rc2 = sm[:, 54 + qs: 55 + qs]
            v3_ = P.op("vector", lambda m2=m2, rc2=rc2: V.reciprocal(rc2, m2), deps=[v2_])
            s2q = sm[:, 58 + qs: 59 + qs]
            v4_ = P.op("vector", lambda rc2=rc2, s2q=s2q: V.tensor_scalar(s2q, rc2, 127.0, None, ALU.mult),
                       deps=[v3_] + R("s2q%d" % qs).wdeps())
            R("s2q%d" % qs).on_write(v4_)
            v5_ = P.op("vector", lambda m2=m2, qs=qs: V.scalar_tensor_tensor(
                sm[:, 62 + qs: 63 + qs], m2, INV127, CW2, ALU.mult, ALU.mult),
                deps=[v2_] + scal_ok + R("rs2c%d" % qs).wdeps())
            R("rs2c%d" % qs).on_write(v5_)
            bq_ = 4 + qs % 2
            tq_ = P.op("tensor", lambda s2q=s2q, qs=qs, bq_=bq_: T.transpose(
                pf[bq_][0:1, 0:128], s2q, identf[:]), deps=[v4_] + bank(bq_).wdeps())
            bank(bq_).on_write(tq_)
            R("s2q%d" % qs).on_read(tq_)
            vq_ = P.op("vector", lambda qs=qs, bq_=bq_: V.tensor_copy(row[0:1, 64 + 128 * qs: 64 + 128 * (qs + 1)],
                                                                     pf[bq_][0:1, 0:128]), deps=[tq_])
            bank(bq_).on_read(vq_)
        tb2 = P.op("tensor", lambda: T.matmul(pf[4][:], onesrf[:], row[0:1, 64:576], start=True, stop=True),
                   deps=[vq_] + bank(4).wdeps())
        bank(4).on_write(tb2)
        xq_last = None
        xq_toks = []
        for ht in range(32):
            jx = ht % 2
            xi = AI[:, XI + 512 * jx: XI + 512 * (jx + 1)]
            vx1 = P.op("vector", lambda ht=ht, xi=xi: V.tensor_tensor(
                xi, hsb[:, ts(ht, 512)], pf[4][:], ALU.mult),
                deps=[tb2] + R("xi%d" % jx).wdeps())
            bank(4).on_read(vx1)
            R("xi%d" % jx).on_write(vx1)
            xq_last = P.op("vector", lambda ht=ht, xi=xi: V.tensor_scalar(
                hsb[:, ts(ht, 512)], xi, 1.0, None, ALU.mult), deps=[vx1])
            R("xi%d" % jx).on_read(xq_last)
            xq_toks.append(xq_last)

        tr_pend = []

        def do_o2tr(dt, e1, o2f):
            for qs in range(4):
                bq_ = 2 + qs % 2
                psl = pf[bq_][0:128, 0:128]
                tq_ = P.op("tensor", lambda o2f=o2f, qs=qs, psl=psl: T.transpose(
                    psl, o2f[:, ts(qs, 128)], identf[:]),
                    deps=[e1] + bank(bq_).wdeps())
                bank(bq_).on_write(tq_)
                R("o2f%d" % (dt % 2)).on_read(tq_)
                vsc = P.op("vector", lambda qs=qs, dt=dt, psl=psl: V.tensor_scalar(
                    AFr[:, O2TM + 1024 * qs + 128 * dt: O2TM + 1024 * qs + 128 * (dt + 1)],
                    psl, sm[:, 62 + qs: 63 + qs], None, ALU.mult),
                    deps=[tq_] + (R("o2tm%d" % qs).wdeps() if dt == 0 else []))
                bank(bq_).on_read(vsc)
                R("rs2c%d" % qs).on_read(vsc)
                if dt == 0:
                    R("o2tm%d" % qs).on_write(vsc)

        for dt in range(8):
            if dt == 4:
                advance_2a(1)
            if dt in (5, 6) and twob_defer:
                twob_defer.pop(0)()
            if dt == 2 and tt + 1 < 8:
                emit_rhsw_load(tt + 1)
                wpre1[0] = load_w1win(0)
                wpre1[1] = load_w1win(1)
            if dt == 7 and tt + 1 < 8:
                wpre2[0] = load_w2win(0)
                wpre2[1] = load_w2win(1)
            jp = dt % 2
            bmm = 0 if jp == 0 else 1
            if dt + 1 < 8:
                wtoks2[2 * dt + 2] = load_w2win(2 * dt + 2)
                wtoks2[2 * dt + 3] = load_w2win(2 * dt + 3)
            tmm = None
            for hc in range(32):
                wv = AB[:, W2W + 2048 * ((2 * dt + hc // 16) % 4): W2W + 2048 * ((2 * dt + hc // 16) % 4 + 1)]
                tmm = P.op("tensor", lambda bmm=bmm, wv=wv, hc=hc: T.matmul(
                    pf[bmm][:], wv[:, 128 * (hc % 16): 128 * (hc % 16 + 1)], hsb[:, ts(hc, 512)],
                    start=(hc == 0), stop=(hc == 31)),
                    deps=([xq_toks[hc], wtoks2[2 * dt + hc // 16]] +
                          (bank(bmm).wdeps() if hc == 0 else [])),
                    inc=(hc == 31))
            bank(bmm).on_write(tmm)
            R("w2w%d" % ((2 * dt) % 4)).on_read(tmm)
            R("w2w%d" % ((2 * dt + 1) % 4)).on_read(tmm)
            o2f = AFr[:, O2F + 512 * jp: O2F + 512 * (jp + 1)]
            e1 = P.op("vector", lambda o2f=o2f, bmm=bmm: V.tensor_copy(o2f, pf[bmm][:]),
                      deps=[tmm] + R("o2f%d" % jp).wdeps())
            bank(bmm).on_read(e1)
            R("o2f%d" % jp).on_write(e1)
            tr_pend.append((dt, e1, o2f))
            if len(tr_pend) > 1:
                do_o2tr(*tr_pend.pop(0))

        while tr_pend:
            do_o2tr(*tr_pend.pop(0))
        o2_done = (P.c["vector"], P.c["vector"].n)

        for qs in range(4):
            jo = qs % 2
            o2tm = AFr[:, O2TM + 1024 * qs: O2TM + 1024 * (qs + 1)]
            x1t = AFr[:, X1T + 1024 * qs: X1T + 1024 * (qs + 1)]
            v1 = P.op("vector", lambda o2tm=o2tm, x1t=x1t: V.tensor_tensor(o2tm, o2tm, x1t, ALU.add),
                      deps=[o2_done] + R("x1t%d" % qs).ready)
            R("x1t%d" % qs).on_read(v1)
            bs = sm[:, 64 + 12 * jo: 76 + 12 * jo]
            mv = sm[:, 88 + 2 * jo: 90 + 2 * jo]
            vb1 = P.op("vector", lambda o2tm=o2tm, bs=bs: V.bn_stats(bs[:, 0:6], o2tm[:, 0:512]),
                       deps=[v1] + R("bs2%d" % jo).wdeps())
            vb2 = P.op("vector", lambda o2tm=o2tm, bs=bs: V.bn_stats(bs[:, 6:12], o2tm[:, 512:1024]),
                       deps=[vb1])
            v4 = P.op("vector", lambda bs=bs, mv=mv: V.bn_aggr(mv, bs), deps=[vb2] + R("mv2%d" % jo).wdeps())
            R("bs2%d" % jo).on_write(vb2)
            R("bs2%d" % jo).on_read(v4)
            R("mv2%d" % jo).on_write(v4)
            sd = sm[:, 92 + jo: 93 + jo]
            s1_ = P.op("scalar", lambda mv=mv, sd=sd: Sc.activation(sd, mv[:, 1:2], AF.Sqrt,
                                                                    bias=CST_EPS, scale=1.0),
                       deps=[v4] + R("sd2%d" % jo).wdeps())
            R("mv2%d" % jo).on_read(s1_)
            R("sd2%d" % jo).on_write(s1_)
            rstd = sm[:, 94 + jo: 95 + jo]
            v5 = P.op("vector", lambda sd=sd, rstd=rstd: V.reciprocal(rstd, sd),
                      deps=[s1_] + R("rstd2%d" % jo).wdeps())
            R("sd2%d" % jo).on_read(v5)
            R("rstd2%d" % jo).on_write(v5)
            nmr = sm[:, 96 + jo: 97 + jo]
            v6 = P.op("vector", lambda mv=mv, rstd=rstd, nmr=nmr: V.scalar_tensor_tensor(
                nmr, mv[:, 0:1], -1.0, rstd, ALU.mult, ALU.mult),
                deps=[v5] + R("nmr2%d" % jo).wdeps())
            R("nmr2%d" % jo).on_write(v6)
            outb = AFr[:, OUTB + 1024 * jo: OUTB + 1024 * (jo + 1)]
            s2_ = P.op("scalar", lambda o2tm=o2tm, outb=outb, rstd=rstd, nmr=nmr: Sc.activation(
                outb, o2tm, AF.Identity, bias=nmr, scale=rstd),
                deps=[v6] + R("outb%d" % jo).wdeps())
            R("rstd2%d" % jo).on_read(s2_)
            R("nmr2%d" % jo).on_read(s2_)
            R("o2tm%d" % qs).on_read(s2_)
            R("outb%d" % jo).on_write(s2_)
            dy = P.dma("sync", y_h[512 * tt + 128 * qs: 512 * tt + 128 * (qs + 1), :], outb, deps=[s2_])
            R("outb%d" % jo).on_read(dy)
        R("hs").users = [(P.c["tensor"], P.c["tensor"].n)]

    if debug:
        P.dma("gpsimd", dbg_d[:, 0:8], scal[:], deps=P.all_tokens())
        P.dma("gpsimd", dbg_d[:, 8:72], sm[:, 0:64], deps=[])
        P.dma("gpsimd", dbg3_d, wp[:], deps=[])
    P.op("sync", lambda: nc.sync.nop(), deps=[(P.d["sync"], P.d["sync"].n),
                                              (P.d["gpsimd"], P.d["gpsimd"].n),
                                              (P.d["scalar"], P.d["scalar"].n)], inc=False)

    block = en(nc.Block())

    def runner(eng):
        def body(e):
            for f in P.streams[eng]:
                f(e)
        return body

    block.sync(runner("sync"))
    block.tensor(runner("tensor"))
    block.scalar(runner("scalar"))
    block.vector(runner("vector"))
    block.gpsimd(runner("gpsimd"))

    nc._exit_stack = stck
    return nc


def kernel(L0_attn_x, L0_attn_Uq, L0_attn_Vq, L0_attn_bq, L0_attn_Uk, L0_attn_Vk,
           L0_attn_bk, L0_attn_Uv, L0_attn_Vv, L0_attn_bv, L0_ffn_w1,
           L0_ffn_conv_w, L0_ffn_conv_b, L0_ffn_w2):
    x = np.ascontiguousarray(np.asarray(L0_attn_x, dtype=np.float32))
    B = x.shape[0]
    common = {
        "Uq": np.ascontiguousarray(np.asarray(L0_attn_Uq, np.float32)),
        "Vq": np.ascontiguousarray(np.asarray(L0_attn_Vq, np.float32)),
        "bq": np.ascontiguousarray(np.asarray(L0_attn_bq, np.float32)),
        "Uk": np.ascontiguousarray(np.asarray(L0_attn_Uk, np.float32)),
        "Vk": np.ascontiguousarray(np.asarray(L0_attn_Vk, np.float32)),
        "bk": np.ascontiguousarray(np.asarray(L0_attn_bk, np.float32)),
        "Uv": np.ascontiguousarray(np.asarray(L0_attn_Uv, np.float32)),
        "Vv": np.ascontiguousarray(np.asarray(L0_attn_Vv, np.float32)),
        "bv": np.ascontiguousarray(np.asarray(L0_attn_bv, np.float32)),
        "w1": np.ascontiguousarray(np.asarray(L0_ffn_w1, np.float32)),
        "w2": np.ascontiguousarray(np.asarray(L0_ffn_w2, np.float32)),
        "cw": np.ascontiguousarray(np.asarray(L0_ffn_conv_w, np.float32).reshape(DHID, 3)),
        "cb": np.ascontiguousarray(np.asarray(L0_ffn_conv_b, np.float32).reshape(32, 128)),
        "identb": np.eye(128, dtype=np.float32).astype(ml_dtypes.bfloat16),
        "identf": np.eye(128, dtype=np.float32),
        "onesrb": np.ones((1, 128), np.float32).astype(ml_dtypes.bfloat16),
        "onesrf": np.ones((1, 128), np.float32),
        "onescb": np.ones((128, 1), np.float32).astype(ml_dtypes.bfloat16),
        "cst": np.tile(np.array([[EPS, -1.0, 127.0, INV127, CLIPMIN, SQ_SAFE, RSQ, INV_WN]],
                                np.float32), (128, 1)),
    }
    nc = build_nc()
    in_maps = [dict(common, x=np.ascontiguousarray(x[b])) for b in range(B)]
    res = run_bass_kernel_spmd(nc, in_maps, core_ids=list(range(B)))
    return np.stack([np.asarray(res.results[b]["y"], dtype=np.float32) for b in range(B)], axis=0)


# version 22 (final)
# speedup vs baseline: 1.0229x; 1.0229x over previous
import contextlib
import numpy as np
import ml_dtypes
import concourse.bass as bass
import concourse.mybir as mybir
from concourse.bass_utils import run_bass_kernel_spmd

F32 = mybir.dt.float32
BF16 = mybir.dt.bfloat16
I32 = mybir.dt.int32
AF = mybir.ActivationFunctionType
ALU = mybir.AluOpType
AX = mybir.AxisListType

SEQ, D, H, RNK = 4096, 1024, 8, 32
DHID = 4096
RSQ = 0.08838834764831845
EPS = 1e-5
INV127 = 1.0 / 127.0
CLIPMIN = 1e-5
SQ_SAFE = 1.0404
INV_WN = 1.0 / (DHID * D)

ENGS = ("sync", "tensor", "scalar", "vector", "gpsimd")


class Chan:
    def __init__(self, en, nc, name):
        self.sem = en(nc.semaphore(name))
        self.n = 0
        self.name = name


class Res:
    __slots__ = ("ready", "users")

    def __init__(self):
        self.ready = []
        self.users = []

    def wdeps(self):
        return self.ready + self.users

    def on_write(self, tok):
        assert tok is not None
        self.ready = [tok]
        self.users = []

    def on_read(self, tok):
        if tok is not None:
            self.users.append(tok)


class Prog:
    serial = False

    def __init__(self, nc, en):
        self.nc = nc
        self.streams = {e: [] for e in ENGS}
        self.c = {e: Chan(en, nc, "c" + e[0] + e[1]) for e in ENGS}
        self.d = {e: Chan(en, nc, "d" + e[0] + e[1]) for e in ("sync", "scalar", "gpsimd")}
        self.waited = {e: {} for e in ENGS}
        self.rs = {}

    def R(self, name):
        if name not in self.rs:
            self.rs[name] = Res()
        return self.rs[name]

    def _emit(self, eng, fn, deps, inc_chan, amt):
        if Prog.serial:
            deps = list(deps) + [(c, c.n) for c in self.c.values()] + \
                   [(c, c.n) for c in self.d.values()]
        w = self.waited[eng]
        fdeps = []
        for c, v in deps:
            if v <= 0 or w.get(c.name, 0) >= v:
                continue
            w[c.name] = v
            fdeps.append((c.sem, v))
        tok = None
        if inc_chan is not None:
            inc_chan.n += amt
            tok = (inc_chan, inc_chan.n)

        def thunk(e, fn=fn, fdeps=tuple(fdeps),
                  sem=(inc_chan.sem if inc_chan is not None else None), amt=amt):
            for s, v in fdeps:
                e.wait_ge(s, v)
            r = fn()
            if sem is not None:
                r.then_inc(sem, amt)
        self.streams[eng].append(thunk)
        return tok

    def op(self, eng, fn, deps=(), inc=True):
        return self._emit(eng, fn, deps, self.c[eng] if inc else None, 1)

    def dma(self, eng, out, in_, deps=(), **kw):
        eh = getattr(self.nc, eng)
        return self._emit(eng, lambda: eh.dma_start(out=out, in_=in_, **kw), deps,
                          self.d[eng], 16)

    def all_tokens(self):
        t = [(c, c.n) for c in self.c.values()]
        t += [(c, c.n) for c in self.d.values()]
        return t


def ts(i, n):
    return slice(i * n, (i + 1) * n)


def build_nc(debug=False, serial=False):
    nc = bass.Bass("TRN2", target_bir_lowering=False)

    x_h = nc.dram_tensor("x", [SEQ, D], F32, kind="ExternalInput").ap()
    Uq_h = nc.dram_tensor("Uq", [H, 128, RNK], F32, kind="ExternalInput").ap()
    Vq_h = nc.dram_tensor("Vq", [H, RNK, 128], F32, kind="ExternalInput").ap()
    bq_h = nc.dram_tensor("bq", [H, 128], F32, kind="ExternalInput").ap()
    Uk_h = nc.dram_tensor("Uk", [H, 128, RNK], F32, kind="ExternalInput").ap()
    Vk_h = nc.dram_tensor("Vk", [H, RNK, 128], F32, kind="ExternalInput").ap()
    bk_h = nc.dram_tensor("bk", [H, 128], F32, kind="ExternalInput").ap()
    Uv_h = nc.dram_tensor("Uv", [H, 128, RNK], F32, kind="ExternalInput").ap()
    Vv_h = nc.dram_tensor("Vv", [H, RNK, 128], F32, kind="ExternalInput").ap()
    bv_h = nc.dram_tensor("bv", [H, 128], F32, kind="ExternalInput").ap()
    w1_h = nc.dram_tensor("w1", [DHID, D], F32, kind="ExternalInput").ap()
    w2_h = nc.dram_tensor("w2", [D, DHID], F32, kind="ExternalInput").ap()
    cw_h = nc.dram_tensor("cw", [DHID, 3], F32, kind="ExternalInput").ap()
    cb_h = nc.dram_tensor("cb", [32, 128], F32, kind="ExternalInput").ap()
    identb_h = nc.dram_tensor("identb", [128, 128], BF16, kind="ExternalInput").ap()
    identf_h = nc.dram_tensor("identf", [128, 128], F32, kind="ExternalInput").ap()
    onesrb_h = nc.dram_tensor("onesrb", [1, 128], BF16, kind="ExternalInput").ap()
    onesrf_h = nc.dram_tensor("onesrf", [1, 128], F32, kind="ExternalInput").ap()
    onescb_h = nc.dram_tensor("onescb", [128, 1], BF16, kind="ExternalInput").ap()
    cst_h = nc.dram_tensor("cst", [128, 8], F32, kind="ExternalInput").ap()
    y_h = nc.dram_tensor("y", [SEQ, D], F32, kind="ExternalOutput").ap()

    dkind = "ExternalOutput" if debug else "Internal"
    dbg_d = nc.dram_tensor("dbg_d", [128, 72], F32, kind=dkind).ap() if debug else None
    dbg2_d = nc.dram_tensor("dbg2_d", [1, 640], F32, kind=dkind).ap() if debug else None
    dbg3_d = nc.dram_tensor("dbg3_d", [128, 256], F32, kind=dkind).ap() if debug else None
    dbg4_d = nc.dram_tensor("dbg4_d", [128, 8192], BF16, kind=dkind).ap() if debug else None
    ao_d = nc.dram_tensor("ao_d", [SEQ, D], BF16, kind=dkind).ap()
    x1_d = nc.dram_tensor("x1_d", [SEQ, D], F32, kind=dkind).ap()
    xq1_d = nc.dram_tensor("xq1_d", [D, SEQ + 2], BF16, kind=dkind).ap()
    w1t_d = nc.dram_tensor("w1t_d", [32, 128, 1024], BF16, kind=dkind).ap()
    w2t_d = nc.dram_tensor("w2t_d", [8, 128, 4096], BF16, kind=dkind).ap()

    stck = contextlib.ExitStack()
    en = stck.enter_context

    AB = en(nc.sbuf_tensor("AB", [128, 58368], BF16))
    AFr = en(nc.sbuf_tensor("AF", [128, 17408], F32))
    AI = en(nc.sbuf_tensor("AI", [128, 2304], I32))

    identb = en(nc.sbuf_tensor("s_identb", [128, 128], BF16))
    identf = en(nc.sbuf_tensor("s_identf", [128, 128], F32))
    onesrb = en(nc.sbuf_tensor("s_onesrb", [1, 128], BF16))
    onesrf = en(nc.sbuf_tensor("s_onesrf", [1, 128], F32))
    onescb = en(nc.sbuf_tensor("s_onescb", [128, 1], BF16))
    cst = en(nc.sbuf_tensor("s_cst", [128, 8], F32))
    scal = en(nc.sbuf_tensor("scal", [128, 8], F32))
    wqt = en(nc.sbuf_tensor("wqt", [128, 256], BF16))
    vkt = en(nc.sbuf_tensor("vkt", [128, 256], BF16))
    vvt = en(nc.sbuf_tensor("vvt", [128, 256], BF16))
    uvt = en(nc.sbuf_tensor("uvt", [32, 1024], BF16))
    gk = en(nc.sbuf_tensor("gk", [128, 8], BF16))
    bTq = en(nc.sbuf_tensor("bTq", [128, 8], F32))
    bTv = en(nc.sbuf_tensor("bTv", [128, 8], F32))
    bqb = en(nc.sbuf_tensor("bqb", [128, 8], BF16))
    bvb = en(nc.sbuf_tensor("bvb", [128, 8], BF16))
    wp = en(nc.sbuf_tensor("wp", [128, 256], F32))
    convA = en(nc.sbuf_tensor("convA", [128, 96], F32))
    convB = en(nc.sbuf_tensor("convB", [128, 32], F32))
    dmask = en(nc.sbuf_tensor("dmask", [128, 128], BF16))
    sm = en(nc.sbuf_tensor("sm", [128, 104], F32))
    ewp = en(nc.sbuf_tensor("ewp", [128, 32], F32))
    row = en(nc.sbuf_tensor("row", [1, 640], F32))
    rowb = en(nc.sbuf_tensor("rowb", [1, 640], BF16))

    for i, v in enumerate([EPS, -1.0, 127.0, INV127, CLIPMIN, SQ_SAFE, RSQ, INV_WN]):
        nc.const_aps.register(F32, float(v), cst[:, i:i + 1])
    CST_EPS = cst[:, 0:1]

    pS = en(nc.psum_tensor("pS", [128, 2048], F32))
    pf45 = [en(nc.psum_tensor("pf%d" % i, [128, 512], F32)) for i in (4, 5)]
    pf = [pS[:, 0:512], pS[:, 512:1024], pS[:, 1024:1536], pS[:, 1536:2048],
          pf45[0][:], pf45[1][:]]
    pmS = en(nc.psum_tensor("pmS", [128, 2048], BF16))
    pm = [pmS[:, 0:1024], pmS[:, 1024:2048]]

    Prog.serial = serial
    P = Prog(nc, en)
    R = P.R
    T, Sc, V, G = nc.tensor, nc.scalar, nc.vector, nc.gpsimd

    BK = ["bk%d" % i for i in range(6)] + ["bm0", "bm1"]

    def bank(i):
        return R(BK[i])

    XTM, KST, QST, TVT, XFM, PTR, AOS, BVR = (0, 16384, 24576, 32768, 41216,
                                              45312, 49408, 50432)
    KREP = [XTM, XTM + 8192]
    QREP = [XTM + 4096, XTM + 12288]
    WQT, WQS1, WQS2 = 51456, 53504, 55552
    HS, RHSW, HRW, ATIN, MACC, MACCN, TB, FMS, W1W, W2W, DIAG = (
        0, 16384, 24608, 26664, 28712, 31272, 29224, 32296, 33320, 37416, 45608)
    FMS4 = 0
    WT2, WT1, ALD = 0, 8192, 10240
    XIN, YB, X1B, O2TM, O2F, OUTB, X1T = 0, 2048, 4096, 6144, 10240, 11264, 13312
    IDX, TI, XI = 0, 128, 1152
    ti32 = AI[:, TI: TI + 1024]

    def kap(h):
        return AB[32 * (h % 4): 32 * (h % 4) + 32, KST + 4096 * (h // 4): KST + 4096 * (h // 4) + 4096]

    def qst(h):
        return AB[32 * (h % 4): 32 * (h % 4) + 32, QST + 4096 * (h // 4): QST + 4096 * (h // 4) + 4096]

    def tvt(h):
        return AB[:, TVT + 1056 * h: TVT + 1056 * (h + 1)]

    tp_state = {"i": 0}

    def transpose_evac_bf(src_ap, dst_ap, deps=(), pslice=None):
        i = tp_state["i"] % 2
        tp_state["i"] += 1
        b = bank(6 + i)
        psl = pm[i][0:src_ap.shape[-1], 0:src_ap.shape[0]] if pslice is None else pslice
        pdim = src_ap.shape[0]
        tk = P.op("tensor", lambda: T.transpose(psl, src_ap,
                                                identb[0:pdim, 0:pdim]),
                  deps=list(deps) + b.wdeps() + P.init_ok)
        b.on_write(tk)
        ev = P.op("vector", lambda: V.tensor_copy(dst_ap, psl), deps=[tk])
        b.on_read(ev)
        return ev

    def transpose_evac_bf4(srcs, dst_ap, deps=(), extra_read=None):
        i = tp_state["i"] % 2
        tp_state["i"] += 1
        b = bank(6 + i)
        n = len(srcs)
        tk = None
        for k, src in enumerate(srcs):
            psl = pm[i][:, 128 * k: 128 * (k + 1)]
            tk = P.op("tensor", lambda psl=psl, src=src: T.transpose(psl, src, identb[:]),
                      deps=(list(deps) + b.wdeps() + P.init_ok) if k == 0 else [],
                      inc=(k == n - 1))
        b.on_write(tk)
        if i == 0:
            ev = P.op("vector",
                      lambda dst_ap=dst_ap, n=n: V.tensor_copy(
                          dst_ap, pm[i][:, 0:128 * n].rearrange("p (c f) -> p c f", c=n)),
                      deps=[tk])
        else:
            ev = P.op("scalar",
                      lambda dst_ap=dst_ap, n=n, i=i: Sc.activation(
                          dst_ap, pm[i][:, 0:128 * n].rearrange("p (c f) -> p c f", c=n),
                          AF.Identity, bias=0.0, scale=1.0),
                      deps=[tk])
        b.on_read(ev)
        if extra_read is not None:
            extra_read(ev)
        return ev

    def transpose_evac_f32(src_ap, dst_ap, bids, idx, deps=(), scale=None):
        b = bids[idx % len(bids)]
        psl = pf[b][0:src_ap.shape[-1], 0:src_ap.shape[0]]
        pdim = src_ap.shape[0]
        tk = P.op("tensor", lambda: T.transpose(psl, src_ap, identf[0:pdim, 0:pdim]),
                  deps=list(deps) + bank(b).wdeps() + P.init_ok)
        bank(b).on_write(tk)
        if scale is None:
            ev = P.op("vector", lambda: V.tensor_copy(dst_ap, psl), deps=[tk])
        else:
            ev = P.op("vector", lambda: V.tensor_scalar(dst_ap, psl, scale, None, ALU.mult), deps=[tk])
        bank(b).on_read(ev)
        return ev

    tk1 = P.dma("sync", identb[:], identb_h)
    tk2 = P.dma("sync", identf[:], identf_h)
    tk3 = P.dma("sync", onesrb[:], onesrb_h)
    tk4 = P.dma("sync", onesrf[:], onesrf_h)
    tk5 = P.dma("sync", onescb[:], onescb_h)
    ctok = P.dma("sync", cst[:], cst_h)
    consts_ok = [ctok]
    init_ok = [tk1, tk2, tk3, tk4, tk5, ctok]
    P.init_ok = init_ok
    cw_tok = P.dma("sync", convA[:].rearrange("p (a b) -> p a b", b=3),
                   cw_h.rearrange("(a p) b -> p a b", p=128))
    cb_sb = AFr[0:32, ALD + 768: ALD + 896]
    tok_cb = P.dma("sync", cb_sb, cb_h)
    bld_q = AFr[0:8, ALD + 896: ALD + 1024]
    bld_v = AFr[0:8, ALD + 1024: ALD + 1152]
    tok_bq = P.dma("sync", bld_q, bq_h)
    tok_bv = P.dma("sync", bld_v, bv_h)

    tg = P.op("gpsimd", lambda: G.iota(AI[:, IDX:IDX + 128], pattern=[[1, 128]],
                                       base=0, channel_multiplier=-1))
    P.op("vector", lambda: V.tensor_scalar(dmask[:], AI[:, IDX:IDX + 128], 0.0, None, ALU.is_equal),
         deps=[tg] + consts_ok)

    wt1 = [AFr[:, WT1 + 1024 * j: WT1 + 1024 * (j + 1)] for j in range(2)]
    wt2 = [AFr[:, WT2 + 4096 * j: WT2 + 4096 * (j + 1)] for j in range(2)]
    awt1 = AFr[:, X1T: X1T + 1024]
    awt2 = AFr[:, X1T: X1T + 4096]
    acc1, acc2, tmpr = sm[:, 0:1], sm[:, 1:2], sm[:, 2:3]
    prev_add = P.op("vector", lambda: V.memset(acc1, 0.0))
    prev_add = P.op("vector", lambda: V.memset(acc2, 0.0), deps=[prev_add])
    for t in range(32):
        j = t % 2
        dst = wt1[j]
        d = P.dma("sync", dst, w1_h[ts(t, 128), :], deps=R("wt1_%d" % j).wdeps())
        R("wt1_%d" % j).on_write(d)
        sA_ = P.op("scalar", lambda dst=dst: Sc.activation(awt1, dst, AF.Abs, bias=0.0, scale=1.0),
                   deps=[d] + R("awt").wdeps())
        R("awt").on_write(sA_)
        R("wt1_%d" % j).on_read(sA_)
        v1 = P.op("vector", lambda: V.tensor_reduce(tmpr, awt1, axis=AX.X, op=ALU.add),
                  deps=[sA_] + R("tmpr").wdeps())
        R("tmpr").on_write(v1)
        R("awt").on_read(v1)
        prev_add = P.op("vector", lambda: V.tensor_tensor(acc1, acc1, tmpr, ALU.add),
                        deps=[v1, prev_add])
        R("tmpr").on_read(prev_add)
    for t in range(8):
        j = t % 2
        dst = wt2[j]
        d = P.dma("sync", dst, w2_h[ts(t, 128), :], deps=R("wt2_%d" % j).wdeps())
        R("wt2_%d" % j).on_write(d)
        sA_ = P.op("scalar", lambda dst=dst: Sc.activation(awt2, dst, AF.Abs, bias=0.0, scale=1.0),
                   deps=[d] + R("awt").wdeps())
        R("awt").on_write(sA_)
        R("wt2_%d" % j).on_read(sA_)
        v1 = P.op("vector", lambda: V.tensor_reduce(tmpr, awt2, axis=AX.X, op=ALU.add),
                  deps=[sA_] + R("tmpr").wdeps())
        R("tmpr").on_write(v1)
        R("awt").on_read(v1)
        vlast = P.op("vector", lambda: V.tensor_tensor(acc2, acc2, tmpr, ALU.add),
                     deps=[v1, prev_add])
        prev_add = vlast
        R("tmpr").on_read(prev_add)
    t1 = P.op("tensor", lambda: T.transpose(pf[0][0:1, 0:128], acc1, identf[:]),
              deps=[vlast] + bank(0).wdeps() + init_ok)
    bank(0).on_write(t1)
    t2 = P.op("tensor", lambda: T.transpose(pf[1][0:1, 0:128], acc2, identf[:]),
              deps=[vlast] + bank(1).wdeps())
    bank(1).on_write(t2)
    v1 = P.op("vector", lambda: V.tensor_reduce(row[0:1, 0:1], pf[0][0:1, 0:128], axis=AX.X, op=ALU.add),
              deps=[t1])
    bank(0).on_read(v1)
    v2 = P.op("vector", lambda: V.tensor_reduce(row[0:1, 1:2], pf[1][0:1, 0:128], axis=AX.X, op=ALU.add),
              deps=[t2])
    bank(1).on_read(v2)
    P.op("vector", lambda: V.tensor_scalar(row[0:1, 0:2], row[0:1, 0:2], INV_WN, CLIPMIN, ALU.mult, ALU.max),
         deps=consts_ok + [v1, v2])
    v3 = (P.c["vector"], P.c["vector"].n)
    tb_ = P.op("tensor", lambda: T.matmul(pf[2][:, 0:2], onesrf[:], row[0:1, 0:2], start=True, stop=True),
               deps=[v3] + bank(2).wdeps() + init_ok)
    bank(2).on_write(tb_)
    v4 = P.op("vector", lambda: V.tensor_copy(scal[:, 0:2], pf[2][:, 0:2]), deps=[tb_])
    bank(2).on_read(v4)
    v4 = P.op("vector", lambda: V.reciprocal(scal[:, 2:3], scal[:, 0:1]), deps=[v4])
    v4 = P.op("vector", lambda: V.reciprocal(scal[:, 3:4], scal[:, 1:2]), deps=[v4])
    scal_ok = [v4]
    CW1, CW2, SW1, SW2 = scal[:, 0:1], scal[:, 1:2], scal[:, 2:3], scal[:, 3:4]

    vA = P.op("vector", lambda: V.tensor_scalar(convA[:], convA[:], CW1, None, ALU.mult),
              deps=[cw_tok] + scal_ok)
    convA_ok = [vA]
    vB = transpose_evac_f32(cb_sb, convB[:], (3, 4), 0, deps=[tok_cb])
    convB_ok = [vB]
    vq_ = transpose_evac_f32(bld_q, bTq[:], (3, 4), 1, deps=[tok_bq])
    vv_ = transpose_evac_f32(bld_v, bTv[:], (3, 4), 0, deps=[tok_bv])
    vq2 = P.op("vector", lambda: V.tensor_copy(bqb[:], bTq[:]), deps=[vq_])
    vv2 = P.op("vector", lambda: V.tensor_copy(bvb[:], bTv[:]), deps=[vv_])
    bqb_ok, bvb_ok = [vq2], [vv2]

    for h in range(H):
        uq = AFr[:, ALD + 0: ALD + 32]
        uk = AFr[:, ALD + 32: ALD + 64]
        uv = AFr[:, ALD + 64: ALD + 96]
        vq = AFr[0:32, ALD + 128: ALD + 256]
        vk = AFr[0:32, ALD + 256: ALD + 384]
        vv = AFr[0:32, ALD + 384: ALD + 512]
        rr = R("ald")
        dq = P.dma("sync", uq, Uq_h[h], deps=rr.wdeps())
        dk = P.dma("sync", uk, Uk_h[h])
        dv = P.dma("sync", uv, Uv_h[h])
        d4 = P.dma("sync", vq, Vq_h[h])
        d5 = P.dma("sync", vk, Vk_h[h])
        d6 = P.dma("sync", vv, Vv_h[h])
        rr.on_write(d6)
        uqb = AB[:, XFM + 0: XFM + 32]
        ukb = AB[:, XFM + 32: XFM + 64]
        uvb = AB[:, XFM + 64: XFM + 96]
        vqb = AB[0:32, XFM + 128: XFM + 256]
        vkb = AB[0:32, XFM + 256: XFM + 384]
        vvb = AB[0:32, XFM + 384: XFM + 512]
        gs = AB[0:32, XFM + 512: XFM + 544]
        rx = R("xfm_scratch")
        c1 = P.op("vector", lambda: V.tensor_copy(uqb, uq), deps=[dq] + rx.wdeps())
        c2 = P.op("vector", lambda: V.tensor_copy(ukb, uk), deps=[dk])
        c3 = P.op("vector", lambda: V.tensor_copy(uvb, uv), deps=[dv])
        c4 = P.op("vector", lambda: V.tensor_copy(vqb, vq), deps=[d4])
        c5 = P.op("vector", lambda: V.tensor_copy(vkb, vk), deps=[d5])
        c6 = P.op("vector", lambda: V.tensor_copy(vvb, vv), deps=[d6])
        rr.on_read(c6)
        rx.on_write(c6)
        tG = P.op("tensor", lambda: T.matmul(pf[5][0:32, 0:32], uqb, ukb, start=True, stop=True),
                  deps=[c1, c2] + bank(5).wdeps())
        bank(5).on_write(tG)
        sG = P.op("scalar", lambda: Sc.activation(gs, pf[5][0:32, 0:32], AF.Identity, bias=0.0, scale=RSQ),
                  deps=[tG] + consts_ok)
        bank(5).on_read(sG)
        rx.on_read(sG)
        tW = P.op("tensor", lambda: T.matmul(pf[5][:, 64:96], vqb, gs, start=True, stop=True),
                  deps=[sG, c4] + bank(5).wdeps())
        bank(5).on_write(tW)
        vW = P.op("vector", lambda h=h: V.tensor_copy(wqt[:, ts(h, 32)], pf[5][:, 64:96]), deps=[tW])
        bank(5).on_read(vW)
        R("wqt%d" % h).on_write(vW)
        e1 = transpose_evac_bf(vkb, vkt[:, ts(h, 32)], deps=[c5])
        R("vkt%d" % h).on_write(e1)
        e2 = transpose_evac_bf(vvb, vvt[:, ts(h, 32)], deps=[c6])
        R("vvt%d" % h).on_write(e2)
        e3 = transpose_evac_bf(uvb, uvt[:, ts(h, 128)], deps=[c3])
        R("uvt%d" % h).on_write(e3)
        rx.on_read(e3)
        tg_ = P.op("tensor", lambda h=h: T.matmul(pf[5][0:32, 96:97], ukb, bqb[:, h:h + 1],
                                                  start=True, stop=True),
                   deps=[c2] + bqb_ok + bank(5).wdeps())
        bank(5).on_write(tg_)
        sg_ = P.op("scalar", lambda h=h: Sc.activation(gk[0:32, h:h + 1], pf[5][0:32, 96:97], AF.Identity,
                                                       bias=0.0, scale=RSQ), deps=[tg_])
        bank(5).on_read(sg_)
        R("gk%d" % h).on_write(sg_)

    gk_deps = [R("gk%d" % h).ready[0] for h in range(H)]
    gk_rep = None
    for b_ in range(1, 4):
        gk_rep = P.dma("sync", gk[32 * b_:32 * b_ + 32, :], gk[0:32, :], deps=gk_deps)
    for h in range(H):
        R("gk%d" % h).on_write(gk_rep)

    xtm = [AB[:, XTM + 1024 * j: XTM + 1024 * (j + 1)] for j in range(16)]
    for half in range(2):
        for j in range(16):
            jj = 16 * half + j
            d = P.dma("gpsimd", xtm[j], x_h[ts(jj, 128), :], deps=R("xtm%d" % j).wdeps())
            R("xtm%d" % j).on_write(d)
        for h in range(H):
            xfm = AB[:, XFM + 2048 * (h % 2): XFM + 2048 * (h % 2 + 1)]
            rxf = R("xfm%d" % (h % 2))
            if h < 2:
                rxf.users += R("xfm_scratch").wdeps()
            for g8 in range(2):
                js = [8 * g8 + c4 for c4 in range(8)]
                dep = sum((R("xtm%d" % j).ready for j in js), []) + (rxf.wdeps() if g8 == 0 else [])

                def mark_reads(ev, js=js):
                    for j in js:
                        R("xtm%d" % j).on_read(ev)
                ev = transpose_evac_bf4(
                    [xtm[j][:, ts(h, 128)] for j in js],
                    xfm[:, 1024 * g8: 1024 * (g8 + 1)].rearrange("p (c f) -> p c f", c=8),
                    deps=dep, extra_read=mark_reads)
            rxf.on_write(ev)
            bb = 32 * (h % 4)
            for q in range(4):
                qq = 4 * half + q
                rhs = xfm[:, ts(q, 512)]
                tk = P.op("tensor", lambda h=h, rhs=rhs, bb=bb: T.matmul(
                    pf[0][bb:bb + 32, :], vkt[:, ts(h, 32)], rhs, start=True, stop=True,
                    tile_position=(0, bb)),
                    deps=rxf.ready + R("vkt%d" % h).ready + bank(0).wdeps())
                bank(0).on_write(tk)
                tv = P.op("tensor", lambda h=h, rhs=rhs, bb=bb: T.matmul(
                    pf[1][bb:bb + 32, :], vvt[:, ts(h, 32)], rhs, start=True, stop=True,
                    tile_position=(0, bb)),
                    deps=R("vvt%d" % h).ready + bank(1).wdeps())
                bank(1).on_write(tv)
                tq = P.op("tensor", lambda h=h, rhs=rhs, bb=bb: T.matmul(
                    pf[2][bb:bb + 32, :], wqt[:, ts(h, 32)], rhs, start=True, stop=True,
                    tile_position=(0, bb)),
                    deps=R("wqt%d" % h).ready + bank(2).wdeps())
                bank(2).on_write(tq)
                ev1 = P.op("vector", lambda h=h, qq=qq, bb=bb: V.tensor_copy(kap(h)[:, ts(qq, 512)],
                                                                             pf[0][bb:bb + 32, :]),
                           deps=[tk])
                bank(0).on_read(ev1)
                rxf.on_read(ev1)
                ev2 = P.op("vector", lambda h=h, qq=qq, bb=bb: V.tensor_copy(qst(h)[:, ts(qq, 512)],
                                                                             pf[2][bb:bb + 32, :]),
                           deps=[tq])
                bank(2).on_read(ev2)
                tvrow = AB[bb:bb + 32, PTR + 512 * (q % 2): PTR + 512 * (q % 2) + 512]
                rtv = R("tvrow%d" % (q % 2))
                ev3 = P.op("vector", lambda tvrow=tvrow, bb=bb: V.tensor_copy(tvrow, pf[1][bb:bb + 32, :]),
                           deps=[tv] + rtv.wdeps())
                bank(1).on_read(ev3)
                rtv.on_write(ev3)
                for i in range(4):
                    kt = 4 * qq + i
                    ib = tp_state["i"] % 2
                    tp_state["i"] += 1
                    bmr = bank(6 + ib)
                    psl = pm[ib][0:128, 0:32]
                    tkt = P.op("tensor", lambda tvrow=tvrow, i=i, bb=bb, psl=psl: T.transpose(
                        psl, tvrow[:, ts(i, 128)], identb[bb:bb + 32, bb:bb + 32],
                        tile_position=(bb, 0)),
                        deps=[ev3] + bmr.wdeps() + P.init_ok)
                    bmr.on_write(tkt)
                    rtv.on_read(tkt)
                    evt = P.op("vector", lambda h=h, kt=kt, psl=psl: V.tensor_copy(
                        tvt(h)[:, 33 * kt: 33 * kt + 32], psl), deps=[tkt])
                    bmr.on_read(evt)
                    P.op("vector", lambda h=h, kt=kt: V.memset(tvt(h)[:, 33 * kt + 32: 33 * kt + 33], 1.0),
                         inc=False)
    tvdone = P.op("vector", lambda: V.memset(sm[:, 63:64], 0.0))
    for h in range(H):
        R("kap%d" % h).on_write((P.c["vector"], P.c["vector"].n))
        R("qst%d" % h).on_write((P.c["vector"], P.c["vector"].n))
        R("tvt%d" % h).on_write(tvdone)

    for h in range(H):
        tw = None
        bb = 32 * (h % 4)
        for kt in range(32):
            tw = P.op("tensor", lambda h=h, kt=kt, bb=bb: T.matmul(
                pf[3][:, kt:kt + 1], kap(h)[:, ts(kt, 128)], gk[bb:bb + 32, h:h + 1],
                start=True, stop=True, tile_position=(bb, 0)),
                deps=(R("kap%d" % h).ready + R("gk%d" % h).ready + bank(3).wdeps()) if kt == 0 else [],
                inc=(kt == 31))
        bank(3).on_write(tw)
        ii = 0
        for nm, stash in (("q", qst(h)), ("k", kap(h))):
            base = 8 if nm == "q" else 16
            for q in range(8):
                jq = ii % 2
                ii += 1
                sqb = AB[bb:bb + 32, XFM + 512 * jq: XFM + 512 * (jq + 1)]
                rsq_ = R("sqb%d" % jq)
                bq4 = 4 + jq
                s1 = P.op("scalar", lambda stash=stash, q=q, sqb=sqb: Sc.activation(
                    sqb, stash[:, ts(q, 512)], AF.Square, bias=0.0, scale=1.0),
                    deps=rsq_.wdeps() + R("xfm0").wdeps() + R("xfm1").wdeps())
                rsq_.on_write(s1)
                tq_ = P.op("tensor", lambda sqb=sqb, bb=bb, bq4=bq4: T.matmul(
                    pf[bq4][0:1, :], onescb[bb:bb + 32, 0:1], sqb,
                    start=True, stop=True, tile_position=(bb, 0)),
                           deps=[s1] + bank(bq4).wdeps() + init_ok)
                bank(bq4).on_write(tq_)
                rsq_.on_read(tq_)
                vr = P.op("vector", lambda base=base, q=q, bq4=bq4: V.tensor_reduce(
                    row[0:1, base + q: base + q + 1], pf[bq4][0:1, :], axis=AX.X, op=ALU.max), deps=[tq_])
                bank(bq4).on_read(vr)
        vlastr = (P.c["vector"], P.c["vector"].n)
        va = P.op("vector", lambda: V.tensor_reduce(row[0:1, 24:25], row[0:1, 8:16], axis=AX.X, op=ALU.max),
                  deps=[vlastr])
        vb = P.op("vector", lambda: V.tensor_reduce(row[0:1, 25:26], row[0:1, 16:24], axis=AX.X, op=ALU.max),
                  deps=[va])
        v4 = P.op("vector", lambda: V.tensor_tensor(row[0:1, 26:27], row[0:1, 24:25], row[0:1, 25:26], ALU.mult),
                  deps=[vb])
        s2 = P.op("scalar", lambda: Sc.activation(row[0:1, 27:28], row[0:1, 26:27], AF.Sqrt,
                                                  bias=0.0, scale=SQ_SAFE), deps=[v4])
        v5 = P.op("vector", lambda: V.tensor_reduce(sm[:, 4:5], pf[3][:, 0:32], axis=AX.X, op=ALU.max,
                                                    apply_absolute_value=True), deps=[tw])
        bank(3).on_read(v5)
        t2_ = P.op("tensor", lambda: T.transpose(pf[4][0:1, 0:128], sm[:, 4:5], identf[:]),
                   deps=[v5] + bank(4).wdeps())
        bank(4).on_write(t2_)
        v6 = P.op("vector", lambda: V.tensor_reduce(row[0:1, 28:29], pf[4][0:1, 0:128], axis=AX.X, op=ALU.max),
                  deps=[t2_])
        bank(4).on_read(v6)
        v7 = P.op("vector", lambda: V.tensor_tensor(row[0:1, 29:30], row[0:1, 27:28], row[0:1, 28:29], ALU.add),
                  deps=[s2, v6])
        v8 = P.op("vector", lambda: V.tensor_scalar(row[0:1, 30:31], row[0:1, 29:30], -1.0, None, ALU.mult),
                  deps=[v7])
        t3_ = P.op("tensor", lambda: T.matmul(pf[4][:, 0:1], onesrf[:], row[0:1, 30:31], start=True, stop=True),
                   deps=[v8] + bank(4).wdeps())
        bank(4).on_write(t3_)
        v9 = P.op("vector", lambda: V.tensor_copy(sm[:, 5:6], pf[4][:, 0:1]), deps=[t3_])
        bank(4).on_read(v9)
        v10 = P.op("vector", lambda h=h: V.tensor_scalar(wp[:, ts(h, 32)], pf[3][:, 0:32], sm[:, 5:6],
                                                         None, ALU.add), deps=[v9])
        bank(3).on_read(v10)
        R("wp%d" % h).on_write(v10)
        if debug and h == 0:
            P.dma("gpsimd", dbg2_d, row[:], deps=[v10])
        sew = P.op("scalar", lambda h=h: Sc.activation(ewp[:, 0:32], wp[:, ts(h, 32)], AF.Exp,
                                                       bias=0.0, scale=1.0),
                   deps=[v10] + R("ewp").wdeps())
        R("ewp").on_write(sew)
        tvl = None
        for kt in range(32):
            tvl = P.op("vector", lambda h=h, kt=kt: V.tensor_scalar(
                tvt(h)[:, 33 * kt: 33 * kt + 33], tvt(h)[:, 33 * kt: 33 * kt + 33],
                ewp[:, kt: kt + 1], None, ALU.mult),
                deps=([sew] + R("tvt%d" % h).ready) if kt == 0 else [sew], inc=(kt == 31))
        R("ewp").on_read(tvl)
        R("tvt%d" % h).on_write(tvl)

    tern = [AB[:, WQT + 1024 * j: WQT + 1024 * (j + 1)] for j in range(2)]
    st1b = [AB[:, WQS1 + 1024 * j: WQS1 + 1024 * (j + 1)] for j in range(2)]
    st2b = AB[:, WQS2: WQS2 + 2048]

    wq_defer = []
    w2_dtok = {}

    def emit_w1_tile(t):
        j = t % 2
        dst = wt1[j]
        d = P.dma("sync", dst, w1_h[ts(t, 128), :], deps=R("wt1_%d" % j).wdeps())
        R("wt1_%d" % j).on_write(d)
        v1 = P.op("vector", lambda dst=dst: V.tensor_scalar(ti32, dst, SW1, None, ALU.mult),
                  deps=[d] + scal_ok + R("ti32").wdeps())
        R("ti32").on_write(v1)
        R("wt1_%d" % j).on_read(v1)
        v2 = P.op("vector", lambda j=j: V.tensor_scalar(tern[j], ti32, 1.0, -1.0, ALU.min, ALU.max),
                  deps=[v1] + R("tern%d" % j).wdeps())
        R("tern%d" % j).on_write(v2)
        R("ti32").on_read(v2)

        def part_b(t=t, j=j, v2=v2):
            rst = R("st1_%d" % j)
            ev = transpose_evac_bf4(
                [tern[j][:, ts(c4, 128)] for c4 in range(8)],
                st1b[j].rearrange("p (c f) -> p c f", c=8),
                deps=[v2] + rst.wdeps(),
                extra_read=R("tern%d" % j).on_read)
            rst.on_write(ev)
            d2 = P.dma("gpsimd", w1t_d[t], st1b[j], deps=[ev])
            rst.on_read(d2)
        wq_defer.append(part_b)

    def emit_w2_cc(t, cc):
        j2 = t % 2
        dst = wt2[j2]
        if cc == 0:
            d0 = P.dma("sync", dst, w2_h[ts(t, 128), :], deps=R("wt2_%d" % j2).wdeps())
            R("wt2_%d" % j2).on_write(d0)
            w2_dtok[t] = d0
        d = w2_dtok[t]
        j = cc % 2
        v1 = P.op("vector", lambda dst=dst, cc=cc: V.tensor_scalar(ti32, dst[:, ts(cc, 1024)], SW2,
                                                                   None, ALU.mult),
                  deps=[d] + scal_ok + R("ti32").wdeps())
        R("ti32").on_write(v1)
        R("wt2_%d" % j2).on_read(v1)
        v2 = P.op("vector", lambda j=j: V.tensor_scalar(tern[j], ti32, 1.0, -1.0, ALU.min, ALU.max),
                  deps=[v1] + R("tern%d" % j).wdeps())
        R("tern%d" % j).on_write(v2)
        R("ti32").on_read(v2)

        def part_b(t=t, cc=cc, j=j, v2=v2):
            rst = R("st2")
            base = 1024 * (cc % 2)
            ev = transpose_evac_bf4(
                [tern[j][:, ts(c4, 128)] for c4 in range(8)],
                st2b[:, base: base + 1024].rearrange("p (c f) -> p c f", c=8),
                deps=[v2] + (rst.wdeps() if cc % 2 == 0 else []),
                extra_read=R("tern%d" % j).on_read)
            if cc % 2 == 1:
                rst.on_write(ev)
                d2 = P.dma("gpsimd", w2t_d[t, :, 2048 * (cc // 2): 2048 * (cc // 2 + 1)], st2b, deps=[ev])
                rst.on_read(d2)
        wq_defer.append(part_b)

    wq_work = []
    for t in range(8):
        for cc in range(4):
            wq_work.append(lambda t=t, cc=cc: emit_w2_cc(t, cc))
    for t in range(32):
        wq_work.append(lambda t=t: emit_w1_tile(t))
    wq_pos = [0]

    def emit_wquant_piece(n=1):
        for _ in range(n):
            if wq_pos[0] < len(wq_work):
                while len(wq_defer) > 1:
                    wq_defer.pop(0)()
                wq_work[wq_pos[0]]()
                wq_pos[0] += 1
                if wq_pos[0] == 32:
                    while wq_defer:
                        wq_defer.pop(0)()
                    w2_done[0] = P.all_tokens()
                if wq_pos[0] == len(wq_work):
                    while wq_defer:
                        wq_defer.pop(0)()
                    wq_done[0] = P.all_tokens()

    xq1_tok = {}
    x1_tok = {}
    twob_defer = []
    wq_done = [[]]
    w2_done = [[]]

    def emit_2a_subtile(stt):
            j = stt % 2
            xin = AFr[:, XIN + 1024 * j: XIN + 1024 * (j + 1)]
            yb = AFr[:, YB + 1024 * j: YB + 1024 * (j + 1)]
            x1b = AFr[:, X1B + 1024 * j: X1B + 1024 * (j + 1)]
            atin = AB[:, ATIN + 1024 * j: ATIN + 1024 * (j + 1)]
            tbb = AB[:, TB + 1024 * j: TB + 1024 * (j + 1)]
            fms4 = AB[:, 20496: 20496 + 4096]
            adeps = [aod_tok[(hh, stt // 4)] for hh in range(H)]
            dx = P.dma("sync", xin, x_h[ts(stt, 128), :], deps=R("xin%d" % j).wdeps() + w2_done[0])
            R("xin%d" % j).on_write(dx)
            da = P.dma("gpsimd", atin, ao_d[ts(stt, 128), :], deps=R("atin%d" % j).wdeps() + adeps)
            R("atin%d" % j).on_write(da)
            v1 = P.op("vector", lambda xin=xin, atin=atin, yb=yb: V.tensor_tensor(yb, xin, atin, ALU.add),
                      deps=[dx, da] + R("yb%d" % j).wdeps())
            R("xin%d" % j).on_read(v1)
            R("atin%d" % j).on_read(v1)
            R("yb%d" % j).on_write(v1)
            bs = sm[:, 8 + 12 * j: 20 + 12 * j]
            mv = sm[:, 32 + 2 * j: 34 + 2 * j]
            vb1 = P.op("vector", lambda yb=yb, bs=bs: V.bn_stats(bs[:, 0:6], yb[:, 0:512]),
                       deps=[v1] + R("bs%d" % j).wdeps())
            vb2 = P.op("vector", lambda yb=yb, bs=bs: V.bn_stats(bs[:, 6:12], yb[:, 512:1024]), deps=[vb1])
            v4 = P.op("vector", lambda bs=bs, mv=mv: V.bn_aggr(mv, bs), deps=[vb2] + R("mv%d" % j).wdeps())
            R("bs%d" % j).on_write(vb2)
            R("bs%d" % j).on_read(v4)
            R("mv%d" % j).on_write(v4)
            sd = sm[:, 36 + j: 37 + j]
            s1_ = P.op("scalar", lambda mv=mv, sd=sd: Sc.activation(sd, mv[:, 1:2], AF.Sqrt, bias=CST_EPS,
                                                                    scale=1.0),
                       deps=[v4] + R("sd%d" % j).wdeps())
            R("mv%d" % j).on_read(s1_)
            R("sd%d" % j).on_write(s1_)
            rstd = sm[:, 38 + j: 39 + j]
            v5 = P.op("vector", lambda sd=sd, rstd=rstd: V.reciprocal(rstd, sd),
                      deps=[s1_] + R("rstd%d" % j).wdeps())
            R("sd%d" % j).on_read(v5)
            R("rstd%d" % j).on_write(v5)
            nmr = sm[:, 40 + j: 41 + j]
            v6 = P.op("vector", lambda mv=mv, rstd=rstd, nmr=nmr: V.scalar_tensor_tensor(
                nmr, mv[:, 0:1], -1.0, rstd, ALU.mult, ALU.mult),
                deps=[v5] + R("nmr%d" % j).wdeps())
            R("nmr%d" % j).on_write(v6)
            s2_ = P.op("scalar", lambda yb=yb, x1b=x1b, rstd=rstd, nmr=nmr: Sc.activation(
                x1b, yb, AF.Identity, bias=nmr, scale=rstd),
                deps=[v6] + R("x1b%d" % j).wdeps())
            R("yb%d" % j).on_read(s2_)
            R("rstd%d" % j).on_read(s2_)
            R("nmr%d" % j).on_read(s2_)
            R("x1b%d" % j).on_write(s2_)
            dx1 = P.dma("sync", x1_d[ts(stt, 128), :], x1b, deps=[s2_])
            R("x1b%d" % j).on_read(dx1)
            x1_tok[stt] = dx1
            m1 = sm[:, 42 + j: 43 + j]
            v7 = P.op("vector", lambda x1b=x1b, m1=m1: V.tensor_reduce(m1, x1b, axis=AX.X, op=ALU.max,
                                                                       apply_absolute_value=True),
                      deps=[s2_] + R("m1%d" % j).wdeps())
            R("m1%d" % j).on_write(v7)
            v8c = P.op("vector", lambda m1=m1: V.tensor_scalar(m1, m1, CLIPMIN, None, ALU.max), deps=[v7])
            rc = sm[:, 44 + j: 45 + j]
            v9c = P.op("vector", lambda m1=m1, rc=rc: V.reciprocal(rc, m1), deps=[v8c])
            s1c = sm[:, 46 + j: 47 + j]
            rs1 = sm[:, 48 + j: 49 + j]
            v10c = P.op("vector", lambda rc=rc, s1c=s1c: V.tensor_scalar(s1c, rc, 127.0, None, ALU.mult),
                        deps=[v9c])
            v11c = P.op("vector", lambda m1=m1, rs1=rs1: V.tensor_scalar(rs1, m1, INV127, None, ALU.mult),
                        deps=[v10c])
            R("m1%d" % j).on_read(v11c)
            v12 = P.op("vector", lambda x1b=x1b, s1c=s1c: V.tensor_scalar(ti32, x1b, s1c, None, ALU.mult),
                       deps=[v11c] + R("ti32").wdeps())
            R("ti32").on_write(v12)
            R("x1b%d" % j).on_read(v12)
            v13 = P.op("vector", lambda tbb=tbb, rs1=rs1: V.tensor_scalar(tbb, ti32, rs1, None, ALU.mult),
                       deps=[v12] + R("tb%d" % j).wdeps())
            R("ti32").on_read(v13)
            R("tb%d" % j).on_write(v13)
            def part_b(stt=stt, j=j, tbb=tbb, fms4=fms4, v13=v13):
                rfm = R("fms4")
                ev = None
                fmr = fms4.rearrange("p (c s) -> p c s", c=8)
                dst = fmr[:, :, 128 * (stt % 4): 128 * (stt % 4) + 128]
                ev = transpose_evac_bf4(
                    [tbb[:, ts(c4, 128)] for c4 in range(8)],
                    dst,
                    deps=[v13] + (rfm.wdeps() if stt % 4 == 0 else []),
                    extra_read=R("tb%d" % j).on_read)
                if stt % 4 == 0:
                    rfm.on_write(ev)
                else:
                    rfm.ready = [ev]
                if stt % 4 == 3:
                    tt_ = stt // 4
                    dxq = P.dma("gpsimd",
                                xq1_d[:, 1 + 512 * tt_: 1 + 512 * (tt_ + 1)].rearrange("(a p) f -> p a f", p=128),
                                fms4.rearrange("p (a f) -> p a f", a=8),
                                deps=[ev])
                    rfm.on_read(dxq)
                    xq1_tok[tt_] = dxq
            twob_defer.append(part_b)


    twoa_next = [0]

    def advance_2a(n):
        for _ in range(n):
            if twoa_next[0] < 32:
                while len(twob_defer) > 1:
                    twob_defer.pop(0)()
                emit_2a_subtile(twoa_next[0])
                twoa_next[0] += 1

    def ensure_2a_tile(t):
        while twoa_next[0] < 4 * (t + 1):
            emit_2a_subtile(twoa_next[0])
            twoa_next[0] += 1
        while t not in xq1_tok and twob_defer:
            twob_defer.pop(0)()

    pt_ring = [AB[:, PTR + 1024 * i: PTR + 1024 * (i + 1)] for i in range(4)]
    xtm_users = [(P.c["tensor"], P.c["tensor"].n), (P.c["vector"], P.c["vector"].n)]

    def emit_reps(h):
        sl = h % 2
        deps = (R("rep%d" % sl).wdeps() + R("kap%d" % h).ready + R("qst%d" % h).ready +
                (xtm_users if h < 2 else []))
        tok = None
        for b_ in range(4):
            tok = P.dma("sync", AB[32 * b_: 32 * b_ + 32, KREP[sl]: KREP[sl] + 4096], kap(h),
                        deps=deps if b_ == 0 else [])
        for b_ in range(4):
            tok = P.dma("sync", AB[32 * b_: 32 * b_ + 32, QREP[sl]: QREP[sl] + 4096], qst(h))
        R("rep%d" % sl).on_write(tok)

    def emit_bvrep(h):
        sl = h % 2
        e4 = transpose_evac_bf(bvb[:, h:h + 1], rowb[0:1, 0:128], deps=bvb_ok)
        tb1 = P.op("tensor", lambda: T.matmul(pf[5][:, 0:128], onesrb[:], rowb[0:1, 0:128],
                                              start=True, stop=True),
                   deps=[e4] + bank(5).wdeps() + init_ok)
        bank(5).on_write(tb1)
        v7 = None
        for qs in range(4):
            v7 = P.op("vector", lambda sl=sl, qs=qs: V.tensor_copy(
                AB[:, BVR + 512 * sl + 128 * qs: BVR + 512 * sl + 128 * (qs + 1)], pf[5][:, 0:128]),
                deps=([tb1] + R("bvr%d" % sl).wdeps()) if qs == 0 else [tb1])
            bank(5).on_read(v7)
        R("bvr%d" % sl).on_write(v7)

    aod_tok = {}
    emit_reps(0)
    wtok = None
    for i in range(16):
        b_ = i % 4
        wtok = P.op("tensor", lambda b_=b_: T.matmul(
            pS[:, 512 * b_: 512 * (b_ + 1)], identb[:], AB[:, XTM: XTM + 512],
            start=True, stop=True),
            deps=(bank(0).wdeps() + bank(1).wdeps() + bank(2).wdeps() + bank(3).wdeps() + init_ok)
            if i == 0 else [], inc=(i == 15))
    if wtok is not None:
        for b_ in range(4):
            bank(b_).on_write(wtok)

    for h in range(H):
        emit_bvrep(h)
        sl = h % 2
        krep = AB[:, KREP[sl]: KREP[sl] + 4096]
        qrep = AB[:, QREP[sl]: QREP[sl] + 4096]
        last_score = None
        norm_pend = []
        for q in range(8):
            if q == 1 and h + 1 < H:
                emit_reps(h + 1)
            if h == 7 and q >= 2:
                advance_2a(4)
            last_pv = None
            prev_exp = None

            def do_pv_group(kg, etok):
                tpv = None
                for i in range(4):
                    kt = 4 * kg + i
                    pts = pt_ring[2 * (kg % 2) + i // 2][:, 512 * (i % 2): 512 * (i % 2 + 1)]
                    tpv = P.op("tensor", lambda h=h, kt=kt, pts=pts: T.matmul(
                        pf[4][0:33, :], tvt(h)[:, 33 * kt: 33 * kt + 33], pts,
                        start=(kt == 0), stop=(kt == 31)),
                        deps=([etok] + R("tvt%d" % h).ready +
                              (bank(4).wdeps() if kt == 0 else [])) if i == 0 else [etok],
                        inc=(i == 3))
                R("ptg%d" % (kg % 2)).on_read(tpv)
                return tpv

            for kg in range(8):
                tlh = [None, None]
                seh = [None, None]
                for half in range(2):
                    bks = (0, 1) if half == 0 else (2, 3)
                    bdeps = bank(bks[0]).wdeps() + bank(bks[1]).wdeps()
                    tl = None
                    for i2 in range(2):
                        i = 2 * half + i2
                        kt = 4 * kg + i
                        sli = pS[:, 512 * i: 512 * (i + 1)]
                        tl = P.op("tensor", lambda q=q, kt=kt, i=i, sli=sli, krep=krep, qrep=qrep: T.matmul(
                            sli, krep[32 * i: 32 * (i + 1), ts(kt, 128)],
                            qrep[32 * i: 32 * (i + 1), 512 * q: 512 * (q + 1)],
                            start=True, stop=True, tile_position=(32 * i, 0)),
                            deps=(bdeps + R("rep%d" % sl).ready) if i2 == 0 else [],
                            inc=(i2 == 1))
                    tlh[half] = tl
                    se = P.op("scalar", lambda kg=kg, half=half: Sc.activation(
                        AB[:, PTR + 2048 * (kg % 2) + 1024 * half: PTR + 2048 * (kg % 2) + 1024 * (half + 1)],
                        pS[:, 1024 * half: 1024 * (half + 1)], AF.Exp, bias=0.0, scale=1.0),
                        deps=[tl] + (R("ptg%d" % (kg % 2)).wdeps() if half == 0 else []))
                    seh[half] = se
                    for b_ in bks:
                        bank(b_).on_write(tl)
                        bank(b_).on_read(se)
                last_score = tlh[1]
                se = seh[1]
                R("ptg%d" % (kg % 2)).on_write(se)
                if prev_exp is not None:
                    last_pv = do_pv_group(kg - 1, prev_exp)
                prev_exp = se
                if kg == 1 and norm_pend:
                    norm_pend.pop(0)()
            last_pv = do_pv_group(7, prev_exp)
            bank(4).on_write(last_pv)
            v1 = P.op("vector", lambda: V.reciprocal(row[0:1, 32:544], pf[4][32:33, :]), deps=[last_pv])
            bank(4).on_read(v1)
            o1f = AFr[0:32, ALD + 512 * (q % 2): ALD + 512 * (q % 2) + 512]
            ro1 = R("o1f%d" % (q % 2))
            ve = P.op("vector", lambda o1f=o1f: V.tensor_copy(o1f, pf[4][0:32, :]),
                      deps=[last_pv] + ro1.wdeps())
            bank(4).on_read(ve)
            ro1.on_write(ve)

            def do_norm(h=h, q=q, sl=sl, o1f=o1f, ro1=ro1, v1=v1, ve=ve):
                v2 = P.op("scalar", lambda: Sc.activation(rowb[0:1, 128:640], row[0:1, 32:544],
                                                          AF.Identity, bias=0.0, scale=1.0), deps=[v1])
                tb1 = P.op("tensor", lambda: T.matmul(pf[5][0:32, :], onesrb[0:1, 0:32], rowb[0:1, 128:640],
                                                      start=True, stop=True),
                           deps=[v2] + bank(5).wdeps())
                bank(5).on_write(tb1)
                rbs = AB[0:32, XFM + 0: XFM + 512]
                o1n = AB[0:32, XFM + 512: XFM + 1024]
                v3 = P.op("scalar", lambda rbs=rbs: Sc.activation(rbs, pf[5][0:32, :],
                                                                 AF.Identity, bias=0.0, scale=1.0),
                          deps=[tb1] + R("sqb0").wdeps())
                bank(5).on_read(v3)
                R("sqb0").on_write(v3)
                v4 = P.op("vector", lambda o1n=o1n, o1f=o1f, rbs=rbs: V.tensor_tensor(o1n, o1f, rbs, ALU.mult),
                          deps=[v3, ve] + R("sqb1").wdeps())
                ro1.on_read(v4)
                R("sqb0").on_read(v4)
                ta = None
                for qs in range(4):
                    ta = P.op("tensor", lambda h=h, qs=qs, o1n=o1n: T.matmul(
                        pf[5][:, ts(qs, 128)], o1n[:, ts(qs, 128)], uvt[:, ts(h, 128)],
                        start=True, stop=True),
                        deps=([v4] + R("uvt%d" % h).ready + bank(5).wdeps()) if qs == 0 else [],
                        inc=(qs == 3))
                bank(5).on_write(ta)
                aos = AB[:, AOS + 512 * (q % 2): AOS + 512 * (q % 2 + 1)]
                ras = R("aos%d" % (q % 2))
                v5 = P.op("vector", lambda aos=aos, sl=sl: V.tensor_tensor(
                    aos, pf[5][:], AB[:, BVR + 512 * sl: BVR + 512 * (sl + 1)], ALU.add),
                    deps=[ta] + R("bvr%d" % sl).ready + ras.wdeps())
                bank(5).on_read(v5)
                ras.on_write(v5)
                dq_ = P.dma("gpsimd",
                            ao_d[512 * q: 512 * (q + 1), ts(h, 128)].rearrange("(a p) f -> p a f", p=128),
                            aos.rearrange("p (a f) -> p a f", a=4),
                            deps=[v5])
                ras.on_read(dq_)
                aod_tok[(h, q)] = dq_
            norm_pend.append(do_norm)
            emit_wquant_piece(1)
        while norm_pend:
            norm_pend.pop(0)()
        R("rep%d" % sl).on_read(last_score)
        R("bvr%d" % sl).on_read((P.c["vector"], P.c["vector"].n))

    emit_wquant_piece(len(wq_work))

    bar = P.all_tokens()
    dtoks = {}
    for ht in range(32):
        vd = None
        for j3 in range(3):
            vd = P.op("vector", lambda ht=ht, j3=j3: V.tensor_scalar(
                AB[:, DIAG + 128 * (3 * ht + j3): DIAG + 128 * (3 * ht + j3 + 1)],
                dmask[:], convA[:, 3 * ht + j3: 3 * ht + j3 + 1], None, ALU.mult),
                deps=(bar + convA_ok) if (ht == 0 and j3 == 0) else [], inc=(j3 == 2))
        dtoks[ht] = vd
    zc = AB[:, MACC: MACC + 1]
    vz = P.op("vector", lambda: V.memset(zc, 0.0), deps=bar)
    zpads = []
    for c in range(8):
        zpads.append(P.dma("gpsimd", xq1_d[ts(c, 128), 0:1], zc, deps=[vz],
                           allow_slow_non_contiguous=True))
        zpads.append(P.dma("gpsimd", xq1_d[ts(c, 128), SEQ + 1: SEQ + 2], zc, deps=[vz],
                           allow_slow_non_contiguous=True))

    bar2 = P.all_tokens()

    hsb = AB[:, HS: HS + 16384]
    macc = AB[:, MACC: MACC + 512]
    maccn = AB[:, MACCN: MACCN + 512]

    def load_w1win(w):
        jw = w % 2
        dst = AB[:, W1W + 2048 * jw: W1W + 2048 * (jw + 1)]
        d = P.dma("scalar", dst.rearrange("p (a f) -> p a f", a=2),
                  w1t_d[2 * w: 2 * w + 2].rearrange("a p f -> p a f"),
                  deps=R("w1w%d" % jw).wdeps() + bar2)
        R("w1w%d" % jw).on_write(d)
        return d

    def load_w2win(w):
        jw = w % 4
        dst = AB[:, W2W + 2048 * jw: W2W + 2048 * (jw + 1)]
        d = P.dma("scalar", dst, w2t_d[w // 2, :, 2048 * (w % 2): 2048 * (w % 2 + 1)],
                  deps=R("w2w%d" % jw).wdeps() + bar2)
        R("w2w%d" % jw).on_write(d)
        return d

    rhsw = AB[:, RHSW: RHSW + 4112]

    def emit_rhsw_load(T):
        rrh = R("rhsw0")
        rdeps = rrh.wdeps() + bar2 + zpads + \
            [xq1_tok[t_] for t_ in (T - 1, T, T + 1) if t_ in xq1_tok]
        rh_engs = ("sync", "gpsimd", "scalar")
        rh_last = {}
        for c in range(8):
            e_ = rh_engs[c % 3]
            rh_last[e_] = P.dma(e_, rhsw[:, 514 * c: 514 * (c + 1)],
                                xq1_d[ts(c, 128), 512 * T: 512 * T + 514],
                                deps=rdeps if c < 3 else [])
        rrh.ready = list(rh_last.values())
        rrh.users = []

    ensure_2a_tile(1)
    emit_rhsw_load(0)
    wpre1, wpre2 = {}, {}
    for tt in range(8):
        ensure_2a_tile(min(tt + 1, 7))
        j = 0
        rrh = R("rhsw0")
        for qs in range(4):
            x1t = AFr[:, X1T + 1024 * qs: X1T + 1024 * (qs + 1)]
            d = P.dma("sync", x1t, x1_d[512 * tt + 128 * qs: 512 * tt + 128 * (qs + 1), :],
                      deps=R("x1t%d" % qs).wdeps() + bar2 + [x1_tok[4 * tt + qs]])
            R("x1t%d" % qs).on_write(d)
        wtoks1 = dict(wpre1)
        wpre1.clear()
        for w in (0, 1):
            if w not in wtoks1:
                wtoks1[w] = load_w1win(w)
        wtoks2 = dict(wpre2)
        wpre2.clear()
        for w in (0, 1):
            if w not in wtoks2:
                wtoks2[w] = load_w2win(w)

        vm = None
        conv_pend = []

        def do_conv(ht, e1, e2, hr):
            bc = 4 + (ht % 2)
            tcv = None
            for j3 in range(3):
                tcv = P.op("tensor", lambda bc=bc, ht=ht, j3=j3, hr=hr: T.matmul(
                    pf[bc][:], AB[:, DIAG + 128 * (3 * ht + j3): DIAG + 128 * (3 * ht + j3 + 1)],
                    hr[:, j3: j3 + 512], start=(j3 == 0), stop=(j3 == 2)),
                    deps=([e1, e2, dtoks[ht]] + bank(bc).wdeps()) if j3 == 0 else [],
                    inc=(j3 == 2))
            bank(bc).on_write(tcv)
            R("hr%d" % (ht % 4)).on_read(tcv)
            ssl = P.op("scalar", lambda ht=ht, bc=bc: Sc.activation(
                hsb[:, ts(ht, 512)], pf[bc][:], AF.Silu, bias=convB[:, ht: ht + 1], scale=1.0),
                deps=[tcv] + convB_ok + (R("hs").wdeps() if ht == 0 else []))
            bank(bc).on_read(ssl)
            if ht == 0:
                R("hs").on_write(ssl)
            return ssl

        def do_macc(hti, ssl):
            global_unused = None
            if hti == 0:
                v_ = P.op("vector", lambda hti=hti: V.tensor_copy(macc, hsb[:, ts(hti, 512)]),
                          deps=[ssl] + R("macc").wdeps())
                R("macc").on_write(v_)
                v_ = P.op("vector", lambda hti=hti: V.tensor_copy(maccn, hsb[:, ts(hti, 512)]),
                          deps=[ssl, v_] + R("maccn").wdeps())
                R("maccn").on_write(v_)
            else:
                v_ = P.op("vector", lambda hti=hti: V.tensor_tensor(
                    macc, macc, hsb[:, ts(hti, 512)], ALU.max), deps=[ssl, prev_vm[0]])
                v_ = P.op("vector", lambda hti=hti: V.tensor_tensor(
                    maccn, maccn, hsb[:, ts(hti, 512)], ALU.min), deps=[ssl, v_])
            prev_vm[0] = v_
            return v_

        prev_vm = [None]
        for ht in range(32):
            if ht in (8, 16, 24):
                advance_2a(1)
            jp = ht % 2
            b0, b1 = (0, 1) if jp == 0 else (2, 3)
            w1w = AB[:, W1W + 2048 * ((ht // 2) % 2): W1W + 2048 * ((ht // 2) % 2 + 1)]
            lhs_off = 1024 * (ht % 2)
            dep0 = (rrh.ready + [wtoks1[ht // 2]] + bank(b0).wdeps() + bank(b1).wdeps())
            tmm = None
            for half in range(2):
                bb = b0 if half == 0 else b1
                for c in range(8):
                    tmm = P.op("tensor", lambda bb=bb, w1w=w1w, lhs_off=lhs_off, c=c, half=half,
                               rhsw=rhsw: T.matmul(
                        pf[bb][:, 0:257],
                        w1w[:, lhs_off + 128 * c: lhs_off + 128 * (c + 1)],
                        rhsw[:, 514 * c + 257 * half: 514 * c + 257 * half + 257],
                        start=(c == 0), stop=(c == 7)),
                        deps=dep0 if (half == 0 and c == 0) else [],
                        inc=(c == 7))
                bank(bb).on_write(tmm)
            rrh.on_read(tmm)
            R("w1w%d" % ((ht // 2) % 2)).on_read(tmm)
            if ht % 2 == 1 and ht // 2 + 2 < 16:
                wtoks1[ht // 2 + 2] = load_w1win(ht // 2 + 2)
            hr = AB[:, HRW + 514 * (ht % 4): HRW + 514 * (ht % 4 + 1)]
            rhr = R("hr%d" % (ht % 4))
            e1 = P.op("scalar", lambda hr=hr, b0=b0: Sc.activation(hr[:, 0:257], pf[b0][:, 0:257],
                                                                   AF.Identity, bias=0.0, scale=1.0),
                      deps=[tmm] + rhr.wdeps())
            bank(b0).on_read(e1)
            e2 = P.op("vector", lambda hr=hr, b1=b1: V.tensor_copy(hr[:, 257:514], pf[b1][:, 0:257]),
                      deps=[tmm] + rhr.wdeps())
            bank(b1).on_read(e2)
            rhr.on_write(e2)
            rhr.users.append(e1)
            conv_pend.append((ht, e1, e2, hr))
            if len(conv_pend) > 1:
                hti, e1i, e2i, hri = conv_pend.pop(0)
                ssl = do_conv(hti, e1i, e2i, hri)
                vm = do_macc(hti, ssl)

        while conv_pend:
            hti, e1i, e2i, hri = conv_pend.pop(0)
            ssl = do_conv(hti, e1i, e2i, hri)
            vm = do_macc(hti, ssl)

        vneg = P.op("vector", lambda: V.tensor_scalar(maccn, maccn, -1.0, None, ALU.mult), deps=[vm])
        vm = P.op("vector", lambda: V.tensor_tensor(macc, macc, maccn, ALU.max), deps=[vneg])
        R("macc").on_write(vm)
        R("maccn").on_read(vm)
        for qs in range(4):
            i2 = qs % 2
            b = bank(6 + i2)
            psl = pm[i2][0:128, 0:128]
            tmc = P.op("tensor", lambda qs=qs, psl=psl: T.transpose(psl, macc[:, ts(qs, 128)], identb[:]),
                       deps=([vm] if qs < 2 else []) + b.wdeps())
            b.on_write(tmc)
            if qs == 3:
                R("macc").on_read(tmc)
            m2 = sm[:, 50 + qs: 51 + qs]
            v1 = P.op("vector", lambda m2=m2, psl=psl: V.tensor_reduce(m2, psl, axis=AX.X, op=ALU.max),
                      deps=[tmc])
            b.on_read(v1)
            v2_ = P.op("vector", lambda m2=m2: V.tensor_scalar(m2, m2, CLIPMIN, None, ALU.max), deps=[v1])
            rc2 = sm[:, 54 + qs: 55 + qs]
            v3_ = P.op("vector", lambda m2=m2, rc2=rc2: V.reciprocal(rc2, m2), deps=[v2_])
            s2q = sm[:, 58 + qs: 59 + qs]
            v4_ = P.op("vector", lambda rc2=rc2, s2q=s2q: V.tensor_scalar(s2q, rc2, 127.0, None, ALU.mult),
                       deps=[v3_] + R("s2q%d" % qs).wdeps())
            R("s2q%d" % qs).on_write(v4_)
            v5_ = P.op("vector", lambda m2=m2, qs=qs: V.scalar_tensor_tensor(
                sm[:, 62 + qs: 63 + qs], m2, INV127, CW2, ALU.mult, ALU.mult),
                deps=[v2_] + scal_ok + R("rs2c%d" % qs).wdeps())
            R("rs2c%d" % qs).on_write(v5_)
            bq_ = 4 + qs % 2
            tq_ = P.op("tensor", lambda s2q=s2q, qs=qs, bq_=bq_: T.transpose(
                pf[bq_][0:1, 0:128], s2q, identf[:]), deps=[v4_] + bank(bq_).wdeps())
            bank(bq_).on_write(tq_)
            R("s2q%d" % qs).on_read(tq_)
            vq_ = P.op("vector", lambda qs=qs, bq_=bq_: V.tensor_copy(row[0:1, 64 + 128 * qs: 64 + 128 * (qs + 1)],
                                                                     pf[bq_][0:1, 0:128]), deps=[tq_])
            bank(bq_).on_read(vq_)
        tb2 = P.op("tensor", lambda: T.matmul(pf[4][:], onesrf[:], row[0:1, 64:576], start=True, stop=True),
                   deps=[vq_] + bank(4).wdeps())
        bank(4).on_write(tb2)
        xq_last = None
        xq_toks = []
        for ht in range(32):
            jx = ht % 2
            xi = AI[:, XI + 512 * jx: XI + 512 * (jx + 1)]
            vx1 = P.op("vector", lambda ht=ht, xi=xi: V.tensor_tensor(
                xi, hsb[:, ts(ht, 512)], pf[4][:], ALU.mult),
                deps=[tb2] + R("xi%d" % jx).wdeps())
            bank(4).on_read(vx1)
            R("xi%d" % jx).on_write(vx1)
            xq_last = P.op("vector", lambda ht=ht, xi=xi: V.tensor_scalar(
                hsb[:, ts(ht, 512)], xi, 1.0, None, ALU.mult), deps=[vx1])
            R("xi%d" % jx).on_read(xq_last)
            xq_toks.append(xq_last)

        tr_pend = []

        def do_o2tr(dt, e1, o2f):
            for qs in range(4):
                bq_ = 2 + qs % 2
                psl = pf[bq_][0:128, 0:128]
                tq_ = P.op("tensor", lambda o2f=o2f, qs=qs, psl=psl: T.transpose(
                    psl, o2f[:, ts(qs, 128)], identf[:]),
                    deps=[e1] + bank(bq_).wdeps())
                bank(bq_).on_write(tq_)
                R("o2f%d" % (dt % 2)).on_read(tq_)
                vsc = P.op("vector", lambda qs=qs, dt=dt, psl=psl: V.tensor_scalar(
                    AFr[:, O2TM + 1024 * qs + 128 * dt: O2TM + 1024 * qs + 128 * (dt + 1)],
                    psl, sm[:, 62 + qs: 63 + qs], None, ALU.mult),
                    deps=[tq_] + (R("o2tm%d" % qs).wdeps() if dt == 0 else []))
                bank(bq_).on_read(vsc)
                R("rs2c%d" % qs).on_read(vsc)
                if dt == 0:
                    R("o2tm%d" % qs).on_write(vsc)

        for dt in range(8):
            if dt == 4:
                advance_2a(1)
            if dt in (5, 6) and twob_defer:
                twob_defer.pop(0)()
            if dt == 2 and tt + 1 < 8:
                emit_rhsw_load(tt + 1)
                wpre1[0] = load_w1win(0)
                wpre1[1] = load_w1win(1)
            if dt == 7 and tt + 1 < 8:
                wpre2[0] = load_w2win(0)
                wpre2[1] = load_w2win(1)
            jp = dt % 2
            bmm = 0 if jp == 0 else 1
            if dt + 1 < 8:
                wtoks2[2 * dt + 2] = load_w2win(2 * dt + 2)
                wtoks2[2 * dt + 3] = load_w2win(2 * dt + 3)
            tmm = None
            for hc in range(32):
                wv = AB[:, W2W + 2048 * ((2 * dt + hc // 16) % 4): W2W + 2048 * ((2 * dt + hc // 16) % 4 + 1)]
                tmm = P.op("tensor", lambda bmm=bmm, wv=wv, hc=hc: T.matmul(
                    pf[bmm][:], wv[:, 128 * (hc % 16): 128 * (hc % 16 + 1)], hsb[:, ts(hc, 512)],
                    start=(hc == 0), stop=(hc == 31)),
                    deps=([xq_toks[hc], wtoks2[2 * dt + hc // 16]] +
                          (bank(bmm).wdeps() if hc == 0 else [])),
                    inc=(hc == 31))
            bank(bmm).on_write(tmm)
            R("w2w%d" % ((2 * dt) % 4)).on_read(tmm)
            R("w2w%d" % ((2 * dt + 1) % 4)).on_read(tmm)
            o2f = AFr[:, O2F + 512 * jp: O2F + 512 * (jp + 1)]
            e1 = P.op("vector", lambda o2f=o2f, bmm=bmm: V.tensor_copy(o2f, pf[bmm][:]),
                      deps=[tmm] + R("o2f%d" % jp).wdeps())
            bank(bmm).on_read(e1)
            R("o2f%d" % jp).on_write(e1)
            tr_pend.append((dt, e1, o2f))
            if len(tr_pend) > 1:
                do_o2tr(*tr_pend.pop(0))

        while tr_pend:
            do_o2tr(*tr_pend.pop(0))
        o2_done = (P.c["vector"], P.c["vector"].n)

        for qs in range(4):
            jo = qs % 2
            o2tm = AFr[:, O2TM + 1024 * qs: O2TM + 1024 * (qs + 1)]
            x1t = AFr[:, X1T + 1024 * qs: X1T + 1024 * (qs + 1)]
            v1 = P.op("vector", lambda o2tm=o2tm, x1t=x1t: V.tensor_tensor(o2tm, o2tm, x1t, ALU.add),
                      deps=[o2_done] + R("x1t%d" % qs).ready)
            R("x1t%d" % qs).on_read(v1)
            bs = sm[:, 64 + 12 * jo: 76 + 12 * jo]
            mv = sm[:, 88 + 2 * jo: 90 + 2 * jo]
            vb1 = P.op("vector", lambda o2tm=o2tm, bs=bs: V.bn_stats(bs[:, 0:6], o2tm[:, 0:512]),
                       deps=[v1] + R("bs2%d" % jo).wdeps())
            vb2 = P.op("vector", lambda o2tm=o2tm, bs=bs: V.bn_stats(bs[:, 6:12], o2tm[:, 512:1024]),
                       deps=[vb1])
            v4 = P.op("vector", lambda bs=bs, mv=mv: V.bn_aggr(mv, bs), deps=[vb2] + R("mv2%d" % jo).wdeps())
            R("bs2%d" % jo).on_write(vb2)
            R("bs2%d" % jo).on_read(v4)
            R("mv2%d" % jo).on_write(v4)
            sd = sm[:, 92 + jo: 93 + jo]
            s1_ = P.op("scalar", lambda mv=mv, sd=sd: Sc.activation(sd, mv[:, 1:2], AF.Sqrt,
                                                                    bias=CST_EPS, scale=1.0),
                       deps=[v4] + R("sd2%d" % jo).wdeps())
            R("mv2%d" % jo).on_read(s1_)
            R("sd2%d" % jo).on_write(s1_)
            rstd = sm[:, 94 + jo: 95 + jo]
            v5 = P.op("vector", lambda sd=sd, rstd=rstd: V.reciprocal(rstd, sd),
                      deps=[s1_] + R("rstd2%d" % jo).wdeps())
            R("sd2%d" % jo).on_read(v5)
            R("rstd2%d" % jo).on_write(v5)
            nmr = sm[:, 96 + jo: 97 + jo]
            v6 = P.op("vector", lambda mv=mv, rstd=rstd, nmr=nmr: V.scalar_tensor_tensor(
                nmr, mv[:, 0:1], -1.0, rstd, ALU.mult, ALU.mult),
                deps=[v5] + R("nmr2%d" % jo).wdeps())
            R("nmr2%d" % jo).on_write(v6)
            outb = AFr[:, OUTB + 1024 * jo: OUTB + 1024 * (jo + 1)]
            s2_ = P.op("scalar", lambda o2tm=o2tm, outb=outb, rstd=rstd, nmr=nmr: Sc.activation(
                outb, o2tm, AF.Identity, bias=nmr, scale=rstd),
                deps=[v6] + R("outb%d" % jo).wdeps())
            R("rstd2%d" % jo).on_read(s2_)
            R("nmr2%d" % jo).on_read(s2_)
            R("o2tm%d" % qs).on_read(s2_)
            R("outb%d" % jo).on_write(s2_)
            dy = P.dma("sync", y_h[512 * tt + 128 * qs: 512 * tt + 128 * (qs + 1), :], outb, deps=[s2_])
            R("outb%d" % jo).on_read(dy)
        R("hs").users = [(P.c["tensor"], P.c["tensor"].n)]

    if debug:
        P.dma("gpsimd", dbg_d[:, 0:8], scal[:], deps=P.all_tokens())
        P.dma("gpsimd", dbg_d[:, 8:72], sm[:, 0:64], deps=[])
        P.dma("gpsimd", dbg3_d, wp[:], deps=[])
    P.op("sync", lambda: nc.sync.nop(), deps=[(P.d["sync"], P.d["sync"].n),
                                              (P.d["gpsimd"], P.d["gpsimd"].n),
                                              (P.d["scalar"], P.d["scalar"].n)], inc=False)

    block = en(nc.Block())

    def runner(eng):
        def body(e):
            for f in P.streams[eng]:
                f(e)
        return body

    block.sync(runner("sync"))
    block.tensor(runner("tensor"))
    block.scalar(runner("scalar"))
    block.vector(runner("vector"))
    block.gpsimd(runner("gpsimd"))

    nc._exit_stack = stck
    return nc


def kernel(L0_attn_x, L0_attn_Uq, L0_attn_Vq, L0_attn_bq, L0_attn_Uk, L0_attn_Vk,
           L0_attn_bk, L0_attn_Uv, L0_attn_Vv, L0_attn_bv, L0_ffn_w1,
           L0_ffn_conv_w, L0_ffn_conv_b, L0_ffn_w2):
    x = np.ascontiguousarray(np.asarray(L0_attn_x, dtype=np.float32))
    B = x.shape[0]
    common = {
        "Uq": np.ascontiguousarray(np.asarray(L0_attn_Uq, np.float32)),
        "Vq": np.ascontiguousarray(np.asarray(L0_attn_Vq, np.float32)),
        "bq": np.ascontiguousarray(np.asarray(L0_attn_bq, np.float32)),
        "Uk": np.ascontiguousarray(np.asarray(L0_attn_Uk, np.float32)),
        "Vk": np.ascontiguousarray(np.asarray(L0_attn_Vk, np.float32)),
        "bk": np.ascontiguousarray(np.asarray(L0_attn_bk, np.float32)),
        "Uv": np.ascontiguousarray(np.asarray(L0_attn_Uv, np.float32)),
        "Vv": np.ascontiguousarray(np.asarray(L0_attn_Vv, np.float32)),
        "bv": np.ascontiguousarray(np.asarray(L0_attn_bv, np.float32)),
        "w1": np.ascontiguousarray(np.asarray(L0_ffn_w1, np.float32)),
        "w2": np.ascontiguousarray(np.asarray(L0_ffn_w2, np.float32)),
        "cw": np.ascontiguousarray(np.asarray(L0_ffn_conv_w, np.float32).reshape(DHID, 3)),
        "cb": np.ascontiguousarray(np.asarray(L0_ffn_conv_b, np.float32).reshape(32, 128)),
        "identb": np.eye(128, dtype=np.float32).astype(ml_dtypes.bfloat16),
        "identf": np.eye(128, dtype=np.float32),
        "onesrb": np.ones((1, 128), np.float32).astype(ml_dtypes.bfloat16),
        "onesrf": np.ones((1, 128), np.float32),
        "onescb": np.ones((128, 1), np.float32).astype(ml_dtypes.bfloat16),
        "cst": np.tile(np.array([[EPS, -1.0, 127.0, INV127, CLIPMIN, SQ_SAFE, RSQ, INV_WN]],
                                np.float32), (128, 1)),
    }
    nc = build_nc()
    in_maps = [dict(common, x=np.ascontiguousarray(x[b])) for b in range(B)]
    res = run_bass_kernel_spmd(nc, in_maps, core_ids=list(range(B)))
    return np.stack([np.asarray(res.results[b]["y"], dtype=np.float32) for b in range(B)], axis=0)
